# version 22
# baseline (speedup 1.0000x reference)
import numpy as np
from contextlib import ExitStack

import concourse.bass as bass
import concourse.mybir as mybir
from concourse.bass_utils import run_bass_kernel_spmd

F32 = mybir.dt.float32
BF16 = mybir.dt.bfloat16
AF = mybir.ActivationFunctionType
ALU = mybir.AluOpType
AX = mybir.AxisListType

B, L, D = 2, 8192, 1024
H, E, W = 16, 64, 128
TPC = 2048
NCH = TPC // 128
XTOK = TPC + 2 * W
XCH = XTOK // 128
NEXP, DFF = 8, 4096
FQ = 1024
NUNIT = NEXP * (DFF // FQ)
QSCALE = float(E) ** -0.5
EPS = 1e-5

ENGS = ("tensor", "vector", "scalar", "sync", "gpsimd")


class Prog:

    def __init__(self):
        self.streams = {e: [] for e in ENGS}
        self.val = {}
        self._lastw = {e: {} for e in ENGS}

    def sem(self, handle):
        self.val[handle] = 0
        return handle

    def op(self, eng, fn, waits=(), inc=None, n=1):
        lw = self._lastw[eng]
        todo = []
        for wv in waits:
            if wv is None:
                continue
            s, v = wv
            if v is None or v <= 0 or lw.get(s, 0) >= v:
                continue
            lw[s] = v
            todo.append((s, v))
        self.streams[eng].append((todo, fn, inc, n))
        if inc is not None:
            self.val[inc] += n
            return self.val[inc]
        return None

    def emit(self, block):
        def run(name):
            def body(eng_obj):
                for waits, fn, inc, n in self.streams[name]:
                    for (s, v) in waits:
                        eng_obj.wait_ge(s, v)
                    bi = fn(eng_obj)
                    if inc is not None:
                        bi.then_inc(inc, n)
            return body
        block.tensor(run("tensor"))
        block.vector(run("vector"))
        block.scalar(run("scalar"))
        block.sync(run("sync"))
        block.gpsimd(run("gpsimd"))


class Bank:
    def __init__(self, ap, release=()):
        self.ap = ap
        self.release = list(release)

    def rw(self):
        return list(self.release)


DEFER_TAILMM_END = False


def build_nc(debug=False):
    nc = bass.Bass("TRN2", target_bir_lowering=False)

    xhd = nc.dram_tensor("xh", [XTOK, D], F32, kind="ExternalInput").ap()
    wqkv = nc.dram_tensor("wqkv", [D, 3 * D], F32, kind="ExternalInput").ap()
    wgd = nc.dram_tensor("wg", [D, NEXP], F32, kind="ExternalInput").ap()
    bgd = nc.dram_tensor("bg", [1, NEXP], F32, kind="ExternalInput").ap()
    b1d = nc.dram_tensor("b1", [NEXP, DFF], F32, kind="ExternalInput").ap()
    b2d = nc.dram_tensor("b2", [NEXP, D], F32, kind="ExternalInput").ap()
    w1d = nc.dram_tensor("w1", [NEXP, D, DFF], F32, kind="ExternalInput").ap()
    w2d = nc.dram_tensor("w2", [NEXP, DFF, D], F32, kind="ExternalInput").ap()
    mld = nc.dram_tensor("maskl", [128, 384], F32, kind="ExternalInput").ap()
    mrd = nc.dram_tensor("maskr", [128, 384], F32, kind="ExternalInput").ap()
    idd = nc.dram_tensor("ident", [128, 128], F32, kind="ExternalInput").ap()
    outd = nc.dram_tensor("out", [TPC, D], F32, kind="ExternalOutput").ap()
    if debug:
        d_xpre = nc.dram_tensor("d_xpre", [TPC, D], F32, kind="ExternalOutput").ap()
        d_x1 = nc.dram_tensor("d_x1", [TPC, D], F32, kind="ExternalOutput").ap()
        d_w = nc.dram_tensor("d_w", [TPC, NEXP], F32, kind="ExternalOutput").ap()
        d_y = nc.dram_tensor("d_y", [TPC, D], F32, kind="ExternalOutput").ap()
        d_u = nc.dram_tensor("d_u", [TPC, NEXP], F32, kind="ExternalOutput").ap()

    P = ExitStack()

    def sb(stk, name, shape, dt):
        return stk.enter_context(nc.sbuf_tensor(name, shape, dt))

    def psum(stk, name, shape):
        return stk.enter_context(nc.psum_tensor(name, shape, F32))

    p = Prog()
    s_x = p.sem(P.enter_context(nc.semaphore("s_x")))
    s_w = p.sem(P.enter_context(nc.semaphore("s_w")))
    s_w2 = p.sem(P.enter_context(nc.semaphore("s_w2")))
    s_mm = p.sem(P.enter_context(nc.semaphore("s_mm")))
    s_dv = p.sem(P.enter_context(nc.semaphore("s_dv")))
    s_ac = p.sem(P.enter_context(nc.semaphore("s_ac")))
    s_st = p.sem(P.enter_context(nc.semaphore("s_st")))

    I_f = sb(P, "I_f", [128, 128], F32)
    I_b = sb(P, "I_b", [128, 128], BF16)
    ones_f = sb(P, "ones_f", [1, 128], F32)
    bg_s = sb(P, "bg_s", [1, NEXP], F32)
    wg_b = sb(P, "wg_b", [128, 8, NEXP], BF16)
    b2_b = sb(P, "b2_b", [NEXP, D], BF16)
    b1T = sb(P, "b1T", [128, 256], F32)
    x1T = sb(P, "x1T", [128, 8, TPC], BF16)
    W1g = sb(P, "W1g", [128, 8, FQ], BF16)
    W2g = sb(P, "W2g", [128, 8, FQ], BF16)
    wgt = sb(P, "wgt", [128, NCH, NEXP], F32)
    stats = sb(P, "stats", [128, 128], F32)
    C_NM = 0
    C_L, C_RI = 4, 36
    C_S1, C_S2 = 68, 70
    C_M, C_EX2, C_VAR, C_VPE, C_SSTD, C_RSTD, C_NMB = 72, 74, 76, 78, 80, 82, 84
    C_NMG, C_M2G, C_SSG, C_RIG = 116, 117, 118, 119
    U8 = stats[:, 88:96]
    G8A = stats[:, 96:104]
    G8B = stats[:, 104:112]

    def stc(c):
        return stats[:, c:c + 1]

    def MM(dst, lhsT, rhs, start, stop, waits=(), inc=None):
        fn = (lambda _d, _l, _r, _s, _e:
              lambda t: t.matmul(_d, _l, _r, start=_s, stop=_e))(dst, lhsT, rhs, start, stop)
        return p.op("tensor", fn, waits=waits, inc=inc)

    def DV(fn, waits=(), inc=None, drain=True):
        r = p.op("vector", fn, waits=waits, inc=inc)
        if drain:
            p.op("vector", lambda v: v.drain())
        return r

    def AC(fn, waits=(), inc=None):
        return p.op("scalar", fn, waits=waits, inc=inc)

    def SY(fn, waits=(), inc=None, n=16):
        w = list(waits)
        if inc is not None:
            w.append((inc, p.val[inc]))
        return p.op("sync", fn, waits=w, inc=inc, n=n)

    def GP(fn, waits=(), inc=None, n=16):
        w = list(waits)
        if inc is not None:
            w.append((inc, p.val[inc]))
        return p.op("gpsimd", fn, waits=w, inc=inc, n=n)

    SY(lambda e: e.dma_start(out=I_f[:], in_=idd), inc=s_x)
    v_bg = SY(lambda e: e.dma_start(out=bg_s[:], in_=bgd), inc=s_x)
    g_ib = GP(lambda e: e.dma_start(out=I_b[:], in_=idd), inc=s_w)
    g_wg = GP(lambda e: e.dma_start(out=wg_b[:], in_=wgd.rearrange("(c p) e -> p c e", p=128)), inc=s_w)
    g_b2 = GP(lambda e: e.dma_start(out=b2_b[:], in_=b2d), inc=s_w)
    v_ones = DV(lambda v: v.memset(ones_f[:], 1.0), inc=s_dv, drain=False)

    stk_b1 = ExitStack()
    b1_s = sb(stk_b1, "b1_s", [NEXP, DFF], F32)
    v_b1 = SY(lambda e: e.dma_start(out=b1_s[:], in_=b1d), inc=s_x)
    stk_pre = ExitStack()
    ps_pre = psum(stk_pre, "ps_pre", [128, 256])
    m_b1t = None
    for c in range(32):
        m_b1t = MM(ps_pre[:, 8 * c:8 * c + 8], b1_s[:, 128 * c:128 * (c + 1)], I_f[0:NEXP, 0:NEXP],
                   start=(c == 0), stop=(c == 31),
                   waits=([(s_x, v_b1)] if c == 0 else []),
                   inc=(s_mm if c == 31 else None))
    b1ev = DV(lambda v: v.tensor_copy(out=b1T[:], in_=ps_pre[:]),
              waits=[(s_mm, m_b1t)], inc=s_dv, drain=False)
    stk_pre.close()
    stk_b1.close()

    stk_A = ExitStack()
    q_b = sb(stk_A, "q_b", [128, 8, TPC], BF16)
    k_b = sb(stk_A, "k_b", [128, 8, XTOK], BF16)
    v_b = sb(stk_A, "v_b", [128, XCH, D], BF16)

    stk_B = ExitStack()
    xch_q = sb(stk_B, "xch_q", [128, 2, D], F32)
    xb_q = sb(stk_B, "xb_q", [128, 2, D], BF16)
    xT_S = sb(stk_B, "xT_S", [128, 2, 8, 512], BF16)
    wq_t = W1g.ap().rearrange("p c (b f) -> p b c f", b=2)
    wv_b = W2g
    ps_tr = [psum(stk_B, f"ps_tr{i}", [128, 1024]) for i in range(2)]
    bk_tr = [Bank(t[:], [(s_dv, b1ev)]) for t in ps_tr]
    ps_qk = [psum(stk_B, f"ps_qk{i}", [128, 512]) for i in range(2)]
    bk_qk = [Bank(t[:], [(s_dv, b1ev)]) for t in ps_qk]

    g_wv = GP(lambda e: e.dma_start(out=wv_b[:], in_=wqkv[:, 2 * D:3 * D].rearrange("(c p) f -> p c f", p=128)),
              waits=[(s_mm, m_b1t)], inc=s_w)

    slices = [(0, 512), (512, 1024), (1024, 1536), (1536, 2048), (2048, 2304)]
    trdone = {}
    cast_v = {}
    wq_used = [None, None]
    xt_used = [None, None]
    wq_i = 0
    grp = 0
    for si, (s0, s1) in enumerate(slices):
        LS = s1 - s0
        sp = si % 2
        jlist = list(range(s0 // 128, s1 // 128))
        xt_ready = None
        for j in jlist:
            bs = j % 2
            wl = [(s_mm, m_b1t)] + ([(s_mm, trdone[j - 2])] if (j - 2) in trdone else [])
            xj = SY((lambda _j, _b: lambda e: e.dma_start(out=xch_q[:, _b, :],
                                                          in_=xhd[128 * _j:128 * (_j + 1), :]))(j, bs),
                    waits=wl, inc=s_x)
            cw = [(s_x, xj)] + ([(s_mm, trdone[j - 2])] if (j - 2) in trdone else [])
            cast_v[j] = DV((lambda _b: lambda v: v.tensor_copy(out=xb_q[:, _b, :], in_=xch_q[:, _b, :]))(bs),
                           waits=cw, inc=s_dv, drain=False)
            bank = bk_tr[bs]
            for dc in range(8):
                half, q4 = dc // 4, dc % 4
                w_ = [(s_dv, cast_v[j]), (s_w, g_ib)] + (bank.rw() if dc == 0 else [])
                trdone[j] = MM(bank.ap[:, 512 * half + 128 * q4:512 * half + 128 * q4 + 128],
                               xb_q[:, bs, 128 * dc:128 * (dc + 1)], I_b[:],
                               start=(q4 == 0), stop=(q4 == 3), waits=w_,
                               inc=(s_mm if dc == 7 else None))
            jr = j * 128 - s0
            evw = [(s_mm, trdone[j])] + ([xt_used[sp]] if xt_used[sp] is not None else [])
            DV((lambda _jr, _bk, _sp: lambda v: v.tensor_copy(out=xT_S[:, _sp, 0:4, _jr:_jr + 128],
                                                              in_=_bk.ap[:, 0:512]))(jr, bank, sp),
               waits=evw, inc=s_dv, drain=False)
            xt_ready = DV((lambda _jr, _bk, _sp: lambda v: v.tensor_copy(out=xT_S[:, _sp, 4:8, _jr:_jr + 128],
                                                                         in_=_bk.ap[:, 512:1024]))(jr, bank, sp),
                          inc=s_dv, drain=False)
            bank.release = [(s_dv, xt_ready)]

        for kind in ("k", "q"):
            if kind == "q":
                c0 = max(s0, 128) - 128
                c1 = min(s1, 128 + TPC) - 128
                if c1 <= c0:
                    continue
                a0, NN = c0 + 128 - s0, c1 - c0
            else:
                a0, NN = 0, LS
            for op4 in range(2):
                col = (0 if kind == "q" else D) + 512 * op4
                wbuf = wq_i % 2
                wsem = s_w if wbuf == 0 else s_w2
                wl = [wq_used[wbuf]] if wq_used[wbuf] is not None else []
                wql = GP((lambda _c, _b: lambda e: e.dma_start(
                         out=wq_t[:, _b], in_=wqkv[:, _c:_c + 512].rearrange("(c p) f -> p c f", p=128)))(col, wbuf),
                         waits=wl, inc=wsem)
                wq_i += 1
                for sub in range(4):
                    oc = 4 * op4 + sub
                    bank = bk_qk[grp % 2]
                    grp += 1
                    gend = None
                    for dc in range(8):
                        w_ = [(wsem, wql), (s_dv, xt_ready)] + (bank.rw() if dc == 0 else [])
                        gend = MM(bank.ap[:, 0:NN], wq_t[:, wbuf, dc, 128 * sub:128 * sub + 128],
                                  xT_S[:, sp, dc, a0:a0 + NN],
                                  start=(dc == 0), stop=(dc == 7), waits=w_,
                                  inc=(s_mm if dc == 7 else None))
                    if kind == "q":
                        eva = AC((lambda _oc, _c0, _n, _bk: lambda s: s.mul(
                                 out=q_b[:, _oc, _c0:_c0 + _n], in_=_bk.ap[:, 0:_n], mul=QSCALE))(oc, c0, NN, bank),
                                 waits=[(s_mm, gend)], inc=s_ac)
                    else:
                        eva = AC((lambda _oc, _s0, _n, _bk: lambda s: s.copy(
                                 out=k_b[:, _oc, _s0:_s0 + _n], in_=_bk.ap[:, 0:_n]))(oc, s0, NN, bank),
                                 waits=[(s_mm, gend)], inc=s_ac)
                    bank.release = [(s_ac, eva)]
                wq_used[wbuf] = (s_mm, gend)

        for j in jlist:
            jr = j * 128 - s0
            for dh in range(2):
                bank = bk_qk[grp % 2]
                grp += 1
                gend = None
                for dc in range(8):
                    w_ = [(s_w, g_wv), (s_dv, xt_ready)] + (bank.rw() if dc == 0 else [])
                    gend = MM(bank.ap[:], xT_S[:, sp, dc, jr:jr + 128], wv_b[:, dc, 512 * dh:512 * (dh + 1)],
                              start=(dc == 0), stop=(dc == 7), waits=w_,
                              inc=(s_mm if dc == 7 else None))
                eva = AC((lambda _j, _dh, _bk: lambda s: s.copy(
                         out=v_b[:, _j, 512 * _dh:512 * (_dh + 1)], in_=_bk.ap[:]))(j, dh, bank),
                         waits=[(s_mm, gend)], inc=s_ac)
                bank.release = [(s_ac, eva)]
        xt_used[sp] = (s_mm, gend)

    qkv_mm_end = p.val[s_mm]
    qkv_ac_end = p.val[s_ac]
    qkv_dv_end = p.val[s_dv]

    def w1_ap(u):
        e, qf = u // 4, u % 4
        return w1d[e, :, FQ * qf:FQ * (qf + 1)].rearrange("(c p) f -> p c f", p=128)

    def w2_ap(u):
        e, qf = u // 4, u % 4
        return w2d[e, FQ * qf:FQ * (qf + 1), :].rearrange("(c p) f -> p c f", p=128)

    wld_w1_0 = GP(lambda e: e.dma_start(out=W1g[:], in_=w1_ap(0)),
                  waits=[(s_mm, qkv_mm_end)], inc=s_w)
    wld_w2_0 = GP(lambda e: e.dma_start(out=W2g[:], in_=w2_ap(0)), inc=s_w2)
    stk_B.close()

    stk_C = ExitStack()
    xch = sb(stk_C, "xch", [128, 2, D], F32)
    xpre = sb(stk_C, "xpre", [128, 2, D], F32)
    attn_sb = sb(stk_C, "attn_sb", [128, 2, D], BF16)
    x1tmp = sb(stk_C, "x1tmp", [128, 2, D], BF16)
    probs_u = sb(stk_C, "probs_u", [128, 3, 384], BF16)
    probsT = sb(stk_C, "probsT", [128, 3, 384], BF16)
    mask_s0 = sb(stk_C, "mask_s0", [128, 384], F32)
    sq_scr = sb(stk_C, "sq_scr", [128, D], F32)
    c_init = [(s_dv, qkv_dv_end), (s_ac, qkv_ac_end)]
    ps_sc = [psum(stk_C, f"ps_sc{i}", [128, 384]) for i in range(2)]
    bk_sc = [Bank(t[:], c_init) for t in ps_sc]
    ps_tm = psum(stk_C, "ps_tm", [128, 512])
    bk_tm = Bank(ps_tm[:], c_init)
    ps_pt = [psum(stk_C, f"ps_pt{i}", [128, 512]) for i in range(2)]
    bk_pt = [Bank(t[:], c_init) for t in ps_pt]
    ps_av = [psum(stk_C, f"ps_av{i}", [128, 512]) for i in range(3)]
    bk_av = [Bank(t[:], c_init) for t in ps_av]

    def avbank(w, h):
        return (4 * (w // 2) + (2 if h >= 8 else 0) + (w % 2)) % 3

    v_mr = SY(lambda e: e.dma_start(out=mask_s0[:], in_=mld),
              waits=[(s_mm, qkv_mm_end), (s_ac, qkv_ac_end)], inc=s_x)
    mask_state = {"v": v_mr, "reload": None}
    xwld = {
        0: SY(lambda e: e.dma_start(out=xch[:, 0, :], in_=xhd[128:256, :]), inc=s_x),
        1: SY(lambda e: e.dma_start(out=xch[:, 1, :], in_=xhd[256:384, :]), inc=s_x),
    }

    steps = []
    for ipair in range(0, NCH, 2):
        for h in range(H):
            steps.append((ipair, h))
            steps.append((ipair + 1, h))
    NST = len(steps)

    sc_v, exp_v, tr_v, av_v, recip_v, evac_v, evpt_v = {}, {}, {}, {}, {}, {}, {}
    stt_v, sq_v, x1tmm_v = {}, {}, {}
    state = {"ug": None, "wtail": None, "dbg_xpre": {}, "dbg_u": None}
    scb = {}

    def lri(w, h):
        return (w % 2) * 16 + h

    def plan_scores(g):
        w, h = steps[g]
        bi = g % 2
        scb[g] = bi
        hp, hb = h // 2, (h % 2) * 64
        w_ = list(bk_sc[bi].rw())
        if g < 2:
            w_ += [(s_ac, qkv_ac_end)]
        sc_v[g] = MM(ps_sc[bi][:],
                     q_b[hb:hb + 64, hp, 128 * w:128 * (w + 1)],
                     k_b[hb:hb + 64, hp, 128 * w:128 * w + 384],
                     start=True, stop=True, waits=w_, inc=s_mm)

    def st_nm_exp(g):
        w, h = steps[g]
        bi = scb[g]
        bp = g % 3
        if w in (0, NCH - 1):
            mv = DV((lambda _b: lambda v: v.tensor_tensor(
                out=ps_sc[_b][:], in0=ps_sc[_b][:], in1=mask_s0[:], op=ALU.add))(bi),
                waits=[(s_mm, sc_v[g]), (s_x, mask_state["v"])], inc=s_dv, drain=True)
            if w == 0 and h == H - 1:
                mask_state["reload"] = mv
        nmw = [(s_mm, sc_v[g])]
        if g - 3 in exp_v:
            nmw.append((s_ac, exp_v[g - 3]))
        nmv = DV((lambda _bi, _bp: lambda v: v.reduce_max(out=stc(C_NM + _bp), in_=ps_sc[_bi][:],
                                                          axis=AX.X, negate=True))(bi, bp),
                 waits=nmw, inc=s_dv, drain=False)
        ew = [(s_dv, nmv)]
        if g - 3 in tr_v:
            ew.append((s_mm, tr_v[g - 3]))
        if (w - 2, h) in recip_v:
            ew.append((s_dv, recip_v[(w - 2, h)]))
        exp_v[g] = AC((lambda _bi, _bp, _s: lambda s: s.activation(
            out=probs_u[:, _bp, :], in_=ps_sc[_bi][:], func=AF.Exp,
            bias=stc(C_NM + _bp), scale=1.0, accum_out=stc(C_L + _s)))(bi, bp, lri(w, h)),
            waits=ew, inc=s_ac)
        bk_sc[bi].release = [(s_ac, exp_v[g])]

    def st_tr(g):
        bp = g % 3
        bt = g % 2
        for c in range(3):
            w_ = [(s_ac, exp_v[g])] + (bk_pt[bt].rw() if c == 0 else [])
            tr_v[g] = MM(ps_pt[bt][:, 128 * c:128 * (c + 1)],
                         probs_u[:, bp, 128 * c:128 * (c + 1)], I_b[:],
                         start=(c == 0), stop=(c == 2), waits=w_,
                         inc=(s_mm if c == 2 else None))

    def st_evpt(g):
        bt = g % 2
        sl = g % 3
        epw = [(s_mm, tr_v[g])]
        if g - 3 in av_v:
            epw.append((s_mm, av_v[g - 3]))
        if g % 2 == 1:
            ev = AC((lambda _b, _s: lambda s: s.copy(out=probsT[:, _s, :], in_=ps_pt[_b][:, 0:384]))(bt, sl),
                    waits=epw, inc=s_ac)
            bk_pt[bt].release = [(s_ac, ev)]
            evpt_v[g] = ("ac", ev)
        else:
            ev = DV((lambda _b, _s: lambda v: v.tensor_copy(out=probsT[:, _s, :], in_=ps_pt[_b][:, 0:384]))(bt, sl),
                    waits=epw, inc=s_dv, drain=False)
            bk_pt[bt].release = [(s_dv, ev)]
            evpt_v[g] = ("dv", ev)

    def st_av(g):
        w, h = steps[g]
        sl = g % 3
        aw = avbank(w, h)
        kind, ev = evpt_v[g]
        evw = (s_ac, ev) if kind == "ac" else (s_dv, ev)
        if h % 8 == 0:
            ug = 4 * (w // 2) + (2 if h >= 8 else 0) + (w % 2) - 3
            if ug >= 0:
                tw = 2 * (ug // 4) + (ug % 2)
                th = 7 if (ug % 4) < 2 else 15
                force_drain(lambda: (tw, th) in evac_v)
        for c in range(3):
            w_ = [evw]
            if h % 8 == 0 and c == 0:
                w_ += bk_av[aw].rw()
            av_v[g] = MM(ps_av[aw][:, 64 * (h % 8):64 * (h % 8) + 64],
                         probsT[:, sl, 128 * c:128 * (c + 1)], v_b[:, w + c, 64 * h:64 * h + 64],
                         start=(h % 8 == 0 and c == 0), stop=(h % 8 == 7 and c == 2),
                         waits=w_, inc=(s_mm if c == 2 else None))
        avg_by_wh[(w, h)] = av_v[g]

    def st_recip(g):
        w, h = steps[g]
        rw_ = [(s_ac, exp_v[g])]
        if (w - 2, h) in evac_v:
            rw_.append((s_dv, evac_v[(w - 2, h)]))
        recip_v[(w, h)] = DV((lambda _s: lambda v: v.reciprocal(
            out=stc(C_RI + _s), in_=stc(C_L + _s)))(lri(w, h)),
            waits=rw_, inc=s_dv, drain=False)

    def plan_evac(w, h):
        aw = avbank(w, h)
        gate = avg_by_wh[(w, 7)] if h < 8 else avg_by_wh[(w, 15)]
        w_ = [(s_mm, gate), (s_dv, recip_v[(w, h)])]
        if h == 0 and (w - 2) in stt_v:
            w_.append((s_dv, stt_v[w - 2]))
        evac_v[(w, h)] = DV((lambda _w, _h, _aw, _s: lambda v: v.tensor_scalar(
            out=attn_sb[:, _w % 2, 64 * _h:64 * _h + 64],
            in0=ps_av[_aw][:, 64 * (_h % 8):64 * (_h % 8) + 64],
            scalar1=stc(C_RI + _s), scalar2=None, op0=ALU.mult))(w, h, aw, lri(w, h)),
            waits=w_, inc=s_dv, drain=False)
        if h % 8 == 7:
            bk_av[aw].release = [(s_dv, evac_v[(w, h)])]

    def plan_stt(w):
        par = w % 2
        w_ = [(s_dv, evac_v[(w, 15)]), (s_x, xwld[w])]
        if (w - 2) in sq_v:
            w_.append((s_ac, sq_v[w - 2]))
        if (w - 2) in state["dbg_xpre"]:
            w_.append((s_x, state["dbg_xpre"][w - 2]))
        stt_v[w] = DV((lambda _p: lambda v: v.scalar_tensor_tensor(
            out=xpre[:, _p, :], in0=attn_sb[:, _p, :], scalar=1.0, in1=xch[:, _p, :],
            op0=ALU.mult, op1=ALU.add, accum_out=stc(C_S1 + _p)))(par),
            waits=w_, inc=s_dv)
        if w + 2 < NCH:
            xwld[w + 2] = SY((lambda _i: lambda e: e.dma_start(
                out=xch[:, _i % 2, :], in_=xhd[128 * (_i + 1):128 * (_i + 2), :]))(w + 2),
                waits=[(s_dv, stt_v[w])], inc=s_x)
        if debug:
            state["dbg_xpre"][w] = SY((lambda _i, _p: lambda e: e.dma_start(
                out=d_xpre[128 * _i:128 * (_i + 1), :], in_=xpre[:, _p, :]))(w, par),
                waits=[(s_dv, stt_v[w])], inc=s_x)

    def plan_tail_stats1(j):
        par = j % 2
        xp = xpre[:, par, :]
        w_ = [(s_dv, stt_v[j])]
        if (j - 1) in sq_v:
            w_.append((s_ac, sq_v[j - 1]))
        sq_v[j] = AC((lambda _xp, _p: lambda s: s.activation(
            out=sq_scr[:], in_=_xp, func=AF.Square, accum_out=stc(C_S2 + _p)))(xp, par),
            waits=w_, inc=s_ac)
        DV((lambda _p: lambda v: v.tensor_scalar(out=stc(C_M + _p), in0=stc(C_S1 + _p), scalar1=1.0 / D,
                                                 scalar2=None, op0=ALU.mult))(par),
           waits=[(s_dv, stt_v[j])], drain=False)

    def plan_tail_stats2(j):
        par = j % 2
        DV((lambda _p: lambda v: v.tensor_scalar(out=stc(C_EX2 + _p), in0=stc(C_S2 + _p), scalar1=1.0 / D,
                                                 scalar2=None, op0=ALU.mult))(par),
           waits=[(s_ac, sq_v[j])])
        DV((lambda _p: lambda v: v.scalar_tensor_tensor(out=stc(C_VAR + _p), in0=stc(C_M + _p),
                                                        scalar=stc(C_M + _p), in1=stc(C_EX2 + _p),
                                                        op0=ALU.mult, op1=ALU.subtract))(par))
        vpe = DV((lambda _p: lambda v: v.tensor_scalar(out=stc(C_VPE + _p), in0=stc(C_VAR + _p),
                                                       scalar1=-1.0, scalar2=EPS,
                                                       op0=ALU.mult, op1=ALU.add))(par),
                 inc=s_dv, drain=False)
        lnv = AC((lambda _p: lambda s: s.activation(out=stc(C_SSTD + _p), in_=stc(C_VPE + _p),
                                                    func=AF.Ln))(par),
                 waits=[(s_dv, vpe)], inc=s_ac)
        tail_rstd[j] = AC((lambda _p: lambda s: s.activation(out=stc(C_RSTD + _p), in_=stc(C_SSTD + _p),
                                                             func=AF.Exp, scale=-0.5))(par),
                          waits=[(s_ac, lnv)], inc=s_ac)

    def plan_tail_stats3(j):
        par = j % 2
        DV((lambda _p: lambda v: v.tensor_scalar(out=stc(C_NMB + _p), in0=stc(C_M + _p),
                                                 scalar1=stc(C_RSTD + _p), scalar2=-1.0,
                                                 op0=ALU.mult, op1=ALU.mult))(par),
           waits=[(s_ac, tail_rstd[j])])
        xp = xpre[:, par, :]
        apw = [(s_mm, x1tmm_v[j - 2])] if (j - 2) in x1tmm_v else []
        if debug and (j - 2) in dbg_x1_v:
            apw.append((s_w, dbg_x1_v[j - 2]))
        x1ap_v[j] = DV((lambda _j, _xp, _p: lambda v: v.tensor_scalar(
            out=x1tmp[:, _j % 2, :], in0=_xp, scalar1=stc(C_RSTD + _p), scalar2=stc(C_NMB + _p),
            op0=ALU.mult, op1=ALU.add))(j, xp, par), waits=apw, inc=s_dv)
        if debug:
            dbg_x1_v[j] = GP((lambda _j: lambda e: e.dma_start(out=d_x1[128 * _j:128 * (_j + 1), :],
                                                               in_=x1tmp[:, _j % 2, :]))(j),
                             waits=[(s_dv, x1ap_v[j])], inc=s_w)

    def gen_tail_mm_pieces(j):
        tm = {}

        def p1():
            mm = None
            for q4 in range(4):
                w_ = [(s_dv, x1ap_v[j])]
                if q4 == 0:
                    w_ += bk_tm.rw()
                    if debug:
                        w_.append((s_w, p.val[s_w]))
                mm = MM(ps_tm[:, 128 * q4:128 * q4 + 128],
                        x1tmp[:, j % 2, 128 * q4:128 * (q4 + 1)], I_b[:],
                        start=(q4 == 0), stop=(q4 == 3), waits=w_,
                        inc=(s_mm if q4 == 3 else None))
            tm["mm1"] = mm

        def p2():
            cast = DV((lambda _j: lambda v: v.tensor_copy(
                out=x1T[:, 0:4, 128 * _j:128 * (_j + 1)], in_=ps_tm[:, 0:512]))(j),
                waits=[(s_mm, tm["mm1"])], inc=s_dv, drain=False)
            bk_tm.release = [(s_dv, cast)]
            x1cast_v[(j, 0)] = cast

        def p3():
            mm = None
            for q4 in range(4):
                w_ = [(s_dv, x1ap_v[j])]
                if q4 == 0:
                    w_ += bk_tm.rw()
                mm = MM(ps_tm[:, 128 * q4:128 * q4 + 128],
                        x1tmp[:, j % 2, 128 * (4 + q4):128 * (5 + q4)], I_b[:],
                        start=(q4 == 0), stop=(q4 == 3), waits=w_,
                        inc=(s_mm if q4 == 3 else None))
            tm["mm2"] = mm
            x1tmm_v[j] = mm

        def p4():
            cast = DV((lambda _j: lambda v: v.tensor_copy(
                out=x1T[:, 4:8, 128 * _j:128 * (_j + 1)], in_=ps_tm[:, 0:512]))(j),
                waits=[(s_mm, tm["mm2"])], inc=s_dv, drain=False)
            bk_tm.release = [(s_dv, cast)]
            x1cast_v[(j, 1)] = cast

        return [p1, p2, p3, p4]

    avg_by_wh = {}
    x1cast_v = {}
    x1ap_v = {}
    dbg_x1_v = {}
    tail_rstd = {}
    pend_evac = []
    pend_tailmm = []
    pend_tmpiece = []
    pend_work = []

    def force_drain(cond):
        while not cond():
            we, he = pend_evac.pop(0)
            plan_evac(we, he)
            if he == 15:
                plan_stt(we)
                plan_tail_stats1(we)
                pend_work.append([2, (lambda _w: lambda: plan_tail_stats2(_w))(we)])
                pend_work.append([4, (lambda _w: lambda: plan_tail_stats3(_w))(we)])

    plan_scores(0)
    plan_scores(1)
    for g in range(NST):
        w, h = steps[g]
        st_nm_exp(g)
        if g + 2 < NST:
            plan_scores(g + 2)
        st_tr(g)
        if g >= 2:
            st_recip(g - 2)
        if g >= 1:
            st_evpt(g - 1)
            st_av(g - 1)
            wp, hp_ = steps[g - 1]
            if hp_ == 7:
                pend_evac += [(wp, hh) for hh in range(8)]
            if hp_ == 15:
                pend_evac += [(wp, hh) for hh in range(8, 16)]
        if mask_state["reload"] is not None and not mask_state.get("reloaded"):
            mask_state["v"] = SY(lambda e: e.dma_start(out=mask_s0[:], in_=mrd),
                                 waits=[(s_dv, mask_state["reload"])], inc=s_x)
            mask_state["reloaded"] = True
        n_done = 0
        while pend_evac and n_done < 3:
            we, he = pend_evac[0]
            if (we, he) not in recip_v:
                break
            plan_evac(we, he)
            pend_evac.pop(0)
            n_done += 1
            if he == 15:
                plan_stt(we)
                plan_tail_stats1(we)
                pend_work.append([2, (lambda _w: lambda: plan_tail_stats2(_w))(we)])
                pend_work.append([4, (lambda _w: lambda: plan_tail_stats3(_w))(we)])
        for item in pend_work:
            item[0] -= 1
        while pend_work and pend_work[0][0] <= 0:
            pend_work.pop(0)[1]()
        if h == 0 and w % 2 == 0 and w >= 2:
            pend_tailmm += [w - 2, w - 1]
        while pend_tailmm and pend_tailmm[0] in x1ap_v:
            pend_tmpiece += gen_tail_mm_pieces(pend_tailmm.pop(0))
        if pend_tmpiece:
            pend_tmpiece.pop(0)()
    st_evpt(NST - 1)
    st_av(NST - 1)
    wlast, hlast = steps[NST - 1]
    pend_evac += [(wlast, hh) for hh in range(8, 16)]
    for g in range(NST - 2, NST):
        if steps[g] not in recip_v:
            st_recip(g)
    while pend_evac:
        we, he = pend_evac.pop(0)
        plan_evac(we, he)
        if he == 15:
            plan_stt(we)
            plan_tail_stats1(we)
            plan_tail_stats2(we)
            plan_tail_stats3(we)
    while pend_work:
        pend_work.pop(0)[1]()
    while pend_tailmm:
        pend_tmpiece += gen_tail_mm_pieces(pend_tailmm.pop(0))
    pend_tailmm += [NCH - 2, NCH - 1]
    while pend_tailmm:
        pend_tmpiece += gen_tail_mm_pieces(pend_tailmm.pop(0))
    while pend_tmpiece:
        pend_tmpiece.pop(0)()

    win_mm_end = p.val[s_mm]
    win_dv_end = p.val[s_dv]
    win_ac_end = p.val[s_ac]
    stk_C.close()

    stk_A.close()
    stk_D = ExitStack()
    wT_b = sb(stk_D, "wT_b", [NEXP, TPC], BF16)
    y_acc = sb(stk_D, "y_acc", [128, NCH, D], F32)
    hpr = sb(stk_D, "hpr", [128, 8, TPC], BF16)
    outb = sb(stk_D, "outb", [128, 2, D], F32)
    wb8 = sb(stk_D, "wb8", [128, NEXP], BF16)
    d_init = [(s_dv, win_dv_end), (s_ac, win_ac_end)]
    ps_h = [psum(stk_D, f"ps_h{i}", [128, 1024]) for i in range(2)]
    bk_h = [Bank(t[:], d_init) for t in ps_h]
    ps_y = [psum(stk_D, f"ps_y{i}", [128, 1024]) for i in range(2)]
    bk_y = [Bank(t[:], d_init) for t in ps_y]

    alias_w = [(s_mm, win_mm_end), (s_dv, win_dv_end), (s_ac, win_ac_end), (s_x, p.val[s_x])]

    wld = {}
    wld["w1", 0] = wld_w1_0
    wld["w2", 0] = wld_w2_0

    CGNM = [116, 112]
    CGM2 = [117, 113]
    CGSS = [118, 114]
    CGRI = [119, 115]
    U8s = [stats[:, 88:96], stats[:, 120:128]]
    wtail_v = {}
    dbg_u_v = {}
    glast = None
    for tq in range(NCH):
        w_ = [(s_dv, x1cast_v[(tq, 0)]), (s_dv, x1cast_v[(tq, 1)])]
        if tq == 0:
            w_ += [(s_x, v_bg), (s_dv, v_ones), (s_ac, win_ac_end),
                   (s_dv, evac_v[(NCH - 1, 15)])]
        MM(ps_y[0][:, 8 * tq:8 * tq + 8], ones_f[:], bg_s[:],
           start=(tq == 0), stop=False, waits=w_)
        for dc in range(8):
            glast = MM(ps_y[0][:, 8 * tq:8 * tq + 8], x1T[:, dc, 128 * tq:128 * (tq + 1)],
                       wg_b[:, dc, :], start=False, stop=(tq == NCH - 1 and dc == 7),
                       waits=[(s_w, g_wg)],
                       inc=(s_mm if (tq == NCH - 1 and dc == 7) else None))
    def emit_gating_resolve():
        nmg = ug = None
        for tq in range(NCH):
            par = tq % 2
            U8p = U8s[par]
            nmg = DV((lambda _t, _p: lambda v: v.reduce_max(out=stc(CGNM[_p]), in_=ps_y[0][:, 8 * _t:8 * _t + 8],
                                                            axis=AX.X, negate=True))(tq, par),
                     waits=[(s_mm, glast)], inc=s_dv, drain=False)
            ugw = [(s_dv, nmg)]
            if debug and (tq - 2) in dbg_u_v:
                ugw.append((s_x, dbg_u_v[tq - 2]))
            ug = AC((lambda _t, _p, _u: lambda s: s.activation(out=_u, in_=ps_y[0][:, 8 * _t:8 * _t + 8],
                                                               func=AF.Exp, bias=stc(CGNM[_p])))(tq, par, U8p),
                    waits=ugw, inc=s_ac)
            DV((lambda _p, _u: lambda v: v.reduce_max(out=stc(CGM2[_p]), in_=_u, axis=AX.X))(par, U8p),
               waits=[(s_ac, ug)])
            DV((lambda _p, _u: lambda v: v.tensor_scalar(out=G8A, in0=_u, scalar1=stc(CGM2[_p]),
                                                         scalar2=None, op0=ALU.is_ge))(par, U8p))
            DV((lambda _u: lambda v: v.scalar_tensor_tensor(out=G8B, in0=G8A, scalar=-1e30, in1=_u,
                                                            op0=ALU.mult, op1=ALU.add))(U8p))
            DV((lambda _p: lambda v: v.reduce_max(out=stc(CGM2[_p]), in_=G8B, axis=AX.X))(par))
            DV((lambda _p, _u: lambda v: v.tensor_scalar(out=G8A, in0=_u, scalar1=stc(CGM2[_p]),
                                                         scalar2=None, op0=ALU.is_ge))(par, U8p))
            DV((lambda _p, _u: lambda v: v.scalar_tensor_tensor(out=G8B, in0=_u, scalar=1.0, in1=G8A,
                                                                op0=ALU.mult, op1=ALU.mult,
                                                                accum_out=stc(CGSS[_p])))(par, U8p))
            DV((lambda _p: lambda v: v.reciprocal(out=stc(CGRI[_p]), in_=stc(CGSS[_p])))(par))
            wtail_v[tq] = DV((lambda _t, _p: lambda v: v.tensor_scalar(
                out=wgt[:, _t, :], in0=G8B, scalar1=stc(CGRI[_p]), scalar2=None, op0=ALU.mult))(tq, par),
                inc=s_dv)
            if debug:
                dbg_u_v[tq] = SY((lambda _t, _u: lambda e: e.dma_start(
                    out=d_u[128 * _t:128 * (_t + 1), :], in_=_u))(tq, U8p),
                    waits=[(s_dv, wtail_v[tq])], inc=s_x)
        bk_y[0].release = [(s_dv, nmg), (s_ac, ug)]

    def emit_wt_yinit():
        wtmm = None
        for i in range(NCH):
            g = i // 8
            cw = alias_w + [(s_dv, wtail_v[i])] + ([(s_mm, wtmm)] if wtmm is not None else [])
            cp = DV((lambda _i: lambda v: v.tensor_copy(out=wb8[:], in_=wgt[:, _i, :]))(i),
                    waits=cw, inc=s_dv, drain=False)
            w_ = [(s_dv, cp)] + bk_y[g].rw()
            wtmm = MM(ps_y[g][0:NEXP, 128 * (i % 8):128 * (i % 8) + 128], wb8[:], I_b[:],
                      start=(i % 4 == 0), stop=(i % 4 == 3), waits=w_, inc=s_mm)
        wt_ev = None
        for g in range(2):
            wt_ev = DV((lambda _g: lambda v: v.tensor_copy(out=wT_b[:, 1024 * _g:1024 * (_g + 1)],
                                                           in_=ps_y[_g][0:NEXP, :]))(g),
                       waits=[(s_mm, wtmm)], inc=s_dv, drain=False)

        for i in range(NCH):
            pp = i % 2
            gend = None
            for dh in range(2):
                w_ = [(s_dv, wt_ev), (s_w, g_b2)] + (bk_y[pp].rw() if dh == 0 else [])
                gend = MM(ps_y[pp][:, 512 * dh:512 * (dh + 1)], wT_b[:, 128 * i:128 * (i + 1)],
                          b2_b[:, 512 * dh:512 * (dh + 1)],
                          start=True, stop=True, waits=w_, inc=(s_mm if dh == 1 else None))
            ev = DV((lambda _i, _pp: lambda v: v.tensor_copy(out=y_acc[:, _i, :], in_=ps_y[_pp][:]))(i, pp),
                    waits=[(s_mm, gend)], inc=s_dv)
            bk_y[pp].release = [(s_dv, ev)]

    tbank = [bk_h[0], bk_y[0]]
    fbank = [bk_h[1], bk_y[1]]
    fin_v, sqf_v, ap_v, st_v, tbk_v = {}, {}, {}, {}, {}
    fpair = {i: (0 if i <= 8 else (1 if (i - 9) % 2 == 0 else 0)) for i in range(NCH)}
    prevu = {}

    def plan_trans_fin(i):
        s = fpair[i]
        pv = prevu.get(s)
        tbk = None
        for dc in range(8):
            half, q4 = dc // 4, dc % 4
            w_ = []
            if dc == 0:
                w_ = tbank[s].rw()
                if pv in sqf_v:
                    w_.append((s_ac, sqf_v[pv]))
            tbk = MM(tbank[s].ap[:, 512 * half + 128 * q4:512 * half + 128 * q4 + 128],
                     x1T[:, dc, 128 * i:128 * (i + 1)], I_b[:],
                     start=(q4 == 0), stop=(q4 == 3), waits=w_,
                     inc=(s_mm if dc == 7 else None))
        tbk_v[i] = tbk
        w_ = [(s_mm, tbk), (s_dv, yev31[i])] + fbank[s].rw()
        if pv in ap_v:
            w_.append((s_dv, ap_v[pv]))
        if pv in sqf_v:
            w_.append((s_ac, sqf_v[pv]))
        fin_v[i] = DV((lambda _i, _s: lambda v: v.scalar_tensor_tensor(
            out=fbank[_s].ap[:], in0=tbank[_s].ap[:], scalar=1.0, in1=y_acc[:, _i, :],
            op0=ALU.mult, op1=ALU.add, accum_out=stc(C_S1 + _i % 2)))(i, s),
            waits=w_, inc=s_dv)
        tbank[s].release = [(s_dv, fin_v[i])]

    def plan_fintail(j):
        s = fpair[j]
        sqf_v[j] = AC((lambda _s, _p: lambda sc_: sc_.activation(
            out=tbank[_s].ap[:], in_=fbank[_s].ap[:], func=AF.Square,
            accum_out=stc(C_S2 + _p)))(s, j % 2),
            waits=[(s_dv, fin_v[j])], inc=s_ac)
        par = j % 2
        DV((lambda _p: lambda v: v.tensor_scalar(out=stc(C_M + _p), in0=stc(C_S1 + _p), scalar1=1.0 / D,
                                                 scalar2=None, op0=ALU.mult))(par),
           waits=[(s_dv, fin_v[j])], drain=False)
        DV((lambda _p: lambda v: v.tensor_scalar(out=stc(C_EX2 + _p), in0=stc(C_S2 + _p), scalar1=1.0 / D,
                                                 scalar2=None, op0=ALU.mult))(par),
           waits=[(s_ac, sqf_v[j])])
        DV((lambda _p: lambda v: v.scalar_tensor_tensor(out=stc(C_VAR + _p), in0=stc(C_M + _p),
                                                        scalar=stc(C_M + _p), in1=stc(C_EX2 + _p),
                                                        op0=ALU.mult, op1=ALU.subtract))(par))
        vpe = DV((lambda _p: lambda v: v.tensor_scalar(out=stc(C_VPE + _p), in0=stc(C_VAR + _p),
                                                       scalar1=-1.0, scalar2=EPS,
                                                       op0=ALU.mult, op1=ALU.add))(par),
                 inc=s_dv, drain=False)
        lnv = AC((lambda _p: lambda sc_: sc_.activation(out=stc(C_SSTD + _p), in_=stc(C_VPE + _p),
                                                        func=AF.Ln))(par),
                 waits=[(s_dv, vpe)], inc=s_ac)
        rstd_ac = AC((lambda _p: lambda sc_: sc_.activation(out=stc(C_RSTD + _p), in_=stc(C_SSTD + _p),
                                                            func=AF.Exp, scale=-0.5))(par),
                     waits=[(s_ac, lnv)], inc=s_ac)
        DV((lambda _p: lambda v: v.tensor_scalar(out=stc(C_NMB + _p), in0=stc(C_M + _p),
                                                 scalar1=stc(C_RSTD + _p), scalar2=-1.0,
                                                 op0=ALU.mult, op1=ALU.mult))(par),
           waits=[(s_ac, rstd_ac)])
        w_ = [(s_st, st_v[j - 2])] if (j - 2) in st_v else []
        ap_v[j] = DV((lambda _s, _j, _p: lambda v: v.tensor_scalar(
            out=outb[:, _j % 2, :], in0=fbank[_s].ap[:], scalar1=stc(C_RSTD + _p),
            scalar2=stc(C_NMB + _p), op0=ALU.mult, op1=ALU.add))(s, j, par),
            waits=w_, inc=s_dv)
        fbank[s].release = [(s_dv, ap_v[j]), (s_ac, sqf_v[j])]
        st_v[j] = SY((lambda _j: lambda e: e.dma_start(out=outd[128 * _j:128 * (_j + 1), :],
                                                       in_=outb[:, _j % 2, :]))(j),
                     waits=[(s_dv, ap_v[j])], inc=s_st)
        prevu[s] = j


    yev31 = {}
    for u in range(NUNIT):
        e, qf = u // 4, u % 4
        gel_pr = {}
        hend = None
        for pr in range(2):
            for fc in range(8):
                bank = bk_h[(pr * 8 + fc) % 2]
                gend = None
                for tt in range(2):
                    tq5 = pr * 2 + tt
                    for dc in range(8):
                        w_ = [(s_w, wld["w1", u])]
                        if tt == 0 and dc == 0:
                            w_ += bank.rw()
                        gend = MM(bank.ap[:, 512 * tt:512 * (tt + 1)],
                                  W1g[:, dc, 128 * fc:128 * (fc + 1)],
                                  x1T[:, dc, 512 * tq5:512 * (tq5 + 1)],
                                  start=(dc == 0), stop=(dc == 7), waits=w_,
                                  inc=(s_mm if (tt == 1 and dc == 7) else None))
                hend = gend
                col = 8 * (qf * 8 + fc) + e
                gel_pr[pr] = AC((lambda _fc, _pr, _bk, _col: lambda s: s.activation(
                    out=hpr[:, _fc, 1024 * _pr:1024 * (_pr + 1)], in_=_bk.ap[:],
                    func=AF.Gelu, bias=b1T[:, _col:_col + 1]))(fc, pr, bank, col),
                    waits=[(s_mm, gend)], inc=s_ac)
                bank.release = [(s_ac, gel_pr[pr])]
        wld["w1", u + 1] = GP((lambda _u: lambda e_: e_.dma_start(out=W1g[:], in_=w1_ap(_u)))(u + 1),
                              waits=[(s_mm, hend)], inc=s_w) if u + 1 < NUNIT else None
        if u == 0:
            emit_gating_resolve()
            emit_wt_yinit()
        yend = None
        for tq in range(NCH):
            pp = tq % 2
            bank = bk_y[pp]
            gend = None
            for dh in range(2):
                for fc in range(8):
                    w_ = [(s_w2, wld["w2", u]), (s_ac, gel_pr[tq // 8])]
                    if dh == 0 and fc == 0:
                        w_ += bank.rw()
                    gend = MM(bank.ap[:, 512 * dh:512 * (dh + 1)],
                              hpr[:, fc, 128 * tq:128 * (tq + 1)],
                              W2g[:, fc, 512 * dh:512 * (dh + 1)],
                              start=(fc == 0), stop=(fc == 7), waits=w_,
                              inc=(s_mm if (dh == 1 and fc == 7) else None))
            yend = gend
            ev = DV((lambda _tq, _pp, _e: lambda v: v.scalar_tensor_tensor(
                out=y_acc[:, _tq, :], in0=ps_y[_pp][:], scalar=wgt[:, _tq, _e:_e + 1],
                in1=y_acc[:, _tq, :], op0=ALU.mult, op1=ALU.add))(tq, pp, e),
                waits=[(s_mm, gend), (s_dv, wtail_v[tq])], inc=s_dv)
            bk_y[pp].release = [(s_dv, ev)]
            if u == NUNIT - 1:
                yev31[tq] = ev
                if tq <= 8:
                    if tq >= 1:
                        plan_fintail(tq - 1)
                    plan_trans_fin(tq)
        if u + 1 < NUNIT:
            wld["w2", u + 1] = GP((lambda _u: lambda e_: e_.dma_start(out=W2g[:], in_=w2_ap(_u)))(u + 1),
                                  waits=[(s_mm, yend)], inc=s_w2)

    y_done = p.val[s_dv]
    if debug:
        for i in range(NCH):
            SY((lambda _i: lambda e: e.dma_start(out=d_y[128 * _i:128 * (_i + 1), :], in_=y_acc[:, _i, :]))(i),
               waits=[(s_dv, y_done)], inc=s_x)
            SY((lambda _i: lambda e: e.dma_start(out=d_w[128 * _i:128 * (_i + 1), :], in_=wgt[:, _i, :]))(i),
               waits=[(s_dv, y_done)], inc=s_x)

    for i in range(9, NCH):
        plan_trans_fin(i)
        plan_fintail(i - 1)
    plan_fintail(NCH - 1)

    final_st = p.val[s_st]
    final_x = p.val[s_x]
    SY(lambda e: e.nop(), waits=[(s_st, final_st), (s_x, final_x)], inc=None)
    if debug:
        GP(lambda e: e.nop(), waits=[(s_w, p.val[s_w])], inc=None)

    stk_D.close()
    with nc.Block() as block:
        p.emit(block)
    P.close()
    return nc


def make_in_maps(L0_attn_x, L0_attn_w_qkv, L0_ffn_Wg, L0_ffn_bg, L0_ffn_W1,
                 L0_ffn_b1, L0_ffn_W2, L0_ffn_b2):
    x = np.ascontiguousarray(np.asarray(L0_attn_x, dtype=np.float32))
    wqkv = np.ascontiguousarray(np.asarray(L0_attn_w_qkv, dtype=np.float32))
    wg = np.ascontiguousarray(np.asarray(L0_ffn_Wg, dtype=np.float32))
    bg = np.ascontiguousarray(np.asarray(L0_ffn_bg, dtype=np.float32)).reshape(1, NEXP)
    w1 = np.ascontiguousarray(np.asarray(L0_ffn_W1, dtype=np.float32))
    b1 = np.ascontiguousarray(np.asarray(L0_ffn_b1, dtype=np.float32))
    w2 = np.ascontiguousarray(np.asarray(L0_ffn_W2, dtype=np.float32))
    b2 = np.ascontiguousarray(np.asarray(L0_ffn_b2, dtype=np.float32))
    ident = np.eye(128, dtype=np.float32)
    in_maps = []
    for c in range(8):
        seq, ch = c // 4, c % 4
        t0 = ch * TPC
        xh = np.zeros((XTOK, D), np.float32)
        lo, hi = t0 - W, t0 + TPC + W
        slo, shi = max(lo, 0), min(hi, L)
        xh[slo - lo: shi - lo] = x[seq, slo:shi]
        maskl = np.zeros((128, 384), np.float32)
        maskr = np.zeros((128, 384), np.float32)
        if ch == 0:
            maskl[:, :128] = -1e9
        if ch == 3:
            maskr[:, 256:] = -1e9
        in_maps.append(dict(xh=xh, wqkv=wqkv, wg=wg, bg=bg, b1=b1, b2=b2,
                            w1=w1, w2=w2, maskl=maskl, maskr=maskr, ident=ident))
    return in_maps


def kernel(L0_attn_x, L0_attn_w_qkv, L0_ffn_Wg, L0_ffn_bg, L0_ffn_W1,
           L0_ffn_b1, L0_ffn_W2, L0_ffn_b2):
    in_maps = make_in_maps(L0_attn_x, L0_attn_w_qkv, L0_ffn_Wg, L0_ffn_bg,
                           L0_ffn_W1, L0_ffn_b1, L0_ffn_W2, L0_ffn_b2)
    nc = build_nc(debug=False)
    res = run_bass_kernel_spmd(nc, in_maps, core_ids=list(range(8)))
    out = np.empty((B, L, D), np.float32)
    for c, r in enumerate(res.results):
        out[c // 4, (c % 4) * TPC:(c % 4 + 1) * TPC] = r["out"]
    return out


# version 23
# speedup vs baseline: 1.0916x; 1.0916x over previous
import numpy as np
from contextlib import ExitStack

import concourse.bass as bass
import concourse.mybir as mybir
from concourse.bass_utils import run_bass_kernel_spmd

F32 = mybir.dt.float32
BF16 = mybir.dt.bfloat16
AF = mybir.ActivationFunctionType
ALU = mybir.AluOpType
AX = mybir.AxisListType

B, L, D = 2, 8192, 1024
H, E, W = 16, 64, 128
TPC = 2048
NCH = TPC // 128
XTOK = TPC + 2 * W
XCH = XTOK // 128
NEXP, DFF = 8, 4096
FQ = 1024
NUNIT = NEXP * (DFF // FQ)
QSCALE = float(E) ** -0.5
EPS = 1e-5

ENGS = ("tensor", "vector", "scalar", "sync", "gpsimd")


class Prog:

    def __init__(self):
        self.streams = {e: [] for e in ENGS}
        self.val = {}
        self._lastw = {e: {} for e in ENGS}

    def sem(self, handle):
        self.val[handle] = 0
        return handle

    def op(self, eng, fn, waits=(), inc=None, n=1):
        lw = self._lastw[eng]
        todo = []
        for wv in waits:
            if wv is None:
                continue
            s, v = wv
            if v is None or v <= 0 or lw.get(s, 0) >= v:
                continue
            lw[s] = v
            todo.append((s, v))
        self.streams[eng].append((todo, fn, inc, n))
        if inc is not None:
            self.val[inc] += n
            return self.val[inc]
        return None

    def emit(self, block):
        def run(name):
            def body(eng_obj):
                for waits, fn, inc, n in self.streams[name]:
                    for (s, v) in waits:
                        eng_obj.wait_ge(s, v)
                    bi = fn(eng_obj)
                    if inc is not None:
                        bi.then_inc(inc, n)
            return body
        block.tensor(run("tensor"))
        block.vector(run("vector"))
        block.scalar(run("scalar"))
        block.sync(run("sync"))
        block.gpsimd(run("gpsimd"))


class Bank:
    def __init__(self, ap, release=()):
        self.ap = ap
        self.release = list(release)

    def rw(self):
        return list(self.release)


DEFER_TAILMM_END = False


def build_nc(debug=False):
    nc = bass.Bass("TRN2", target_bir_lowering=False)

    xhd = nc.dram_tensor("xh", [XTOK, D], F32, kind="ExternalInput").ap()
    wqkv = nc.dram_tensor("wqkv", [D, 3 * D], F32, kind="ExternalInput").ap()
    wgd = nc.dram_tensor("wg", [D, NEXP], F32, kind="ExternalInput").ap()
    bgd = nc.dram_tensor("bg", [1, NEXP], F32, kind="ExternalInput").ap()
    b1d = nc.dram_tensor("b1", [NEXP, DFF], F32, kind="ExternalInput").ap()
    b2d = nc.dram_tensor("b2", [NEXP, D], F32, kind="ExternalInput").ap()
    w1d = nc.dram_tensor("w1", [NEXP, D, DFF], F32, kind="ExternalInput").ap()
    w2d = nc.dram_tensor("w2", [NEXP, DFF, D], F32, kind="ExternalInput").ap()
    mld = nc.dram_tensor("maskl", [128, 384], F32, kind="ExternalInput").ap()
    mrd = nc.dram_tensor("maskr", [128, 384], F32, kind="ExternalInput").ap()
    idd = nc.dram_tensor("ident", [128, 128], F32, kind="ExternalInput").ap()
    outd = nc.dram_tensor("out", [TPC, D], F32, kind="ExternalOutput").ap()
    if debug:
        d_xpre = nc.dram_tensor("d_xpre", [TPC, D], F32, kind="ExternalOutput").ap()
        d_x1 = nc.dram_tensor("d_x1", [TPC, D], F32, kind="ExternalOutput").ap()
        d_w = nc.dram_tensor("d_w", [TPC, NEXP], F32, kind="ExternalOutput").ap()
        d_y = nc.dram_tensor("d_y", [TPC, D], F32, kind="ExternalOutput").ap()
        d_u = nc.dram_tensor("d_u", [TPC, NEXP], F32, kind="ExternalOutput").ap()

    P = ExitStack()

    def sb(stk, name, shape, dt):
        return stk.enter_context(nc.sbuf_tensor(name, shape, dt))

    def psum(stk, name, shape):
        return stk.enter_context(nc.psum_tensor(name, shape, F32))

    p = Prog()
    s_x = p.sem(P.enter_context(nc.semaphore("s_x")))
    s_w = p.sem(P.enter_context(nc.semaphore("s_w")))
    s_w2 = p.sem(P.enter_context(nc.semaphore("s_w2")))
    s_mm = p.sem(P.enter_context(nc.semaphore("s_mm")))
    s_dv = p.sem(P.enter_context(nc.semaphore("s_dv")))
    s_ac = p.sem(P.enter_context(nc.semaphore("s_ac")))
    s_st = p.sem(P.enter_context(nc.semaphore("s_st")))

    I_f = sb(P, "I_f", [128, 128], F32)
    I_b = sb(P, "I_b", [128, 128], BF16)
    ones_f = sb(P, "ones_f", [1, 128], F32)
    bg_s = sb(P, "bg_s", [1, NEXP], F32)
    wg_b = sb(P, "wg_b", [128, 8, NEXP], BF16)
    b2_b = sb(P, "b2_b", [NEXP, D], BF16)
    b1T = sb(P, "b1T", [128, 256], F32)
    x1T = sb(P, "x1T", [128, 8, TPC], BF16)
    W1g = sb(P, "W1g", [128, 8, FQ], BF16)
    W2g = sb(P, "W2g", [128, 8, FQ], BF16)
    wgt = sb(P, "wgt", [128, NCH, NEXP], F32)
    stats = sb(P, "stats", [128, 128], F32)
    C_NM = 0
    C_L, C_RI = 4, 36
    C_S1, C_S2 = 68, 70
    C_M, C_EX2, C_VAR, C_VPE, C_SSTD, C_RSTD, C_NMB = 72, 74, 76, 78, 80, 82, 84
    C_NMG, C_M2G, C_SSG, C_RIG = 116, 117, 118, 119
    U8 = stats[:, 88:96]
    G8A = stats[:, 96:104]
    G8B = stats[:, 104:112]

    def stc(c):
        return stats[:, c:c + 1]

    def MM(dst, lhsT, rhs, start, stop, waits=(), inc=None):
        fn = (lambda _d, _l, _r, _s, _e:
              lambda t: t.matmul(_d, _l, _r, start=_s, stop=_e))(dst, lhsT, rhs, start, stop)
        return p.op("tensor", fn, waits=waits, inc=inc)

    def DV(fn, waits=(), inc=None, drain=True):
        r = p.op("vector", fn, waits=waits, inc=inc)
        if drain:
            p.op("vector", lambda v: v.drain())
        return r

    def AC(fn, waits=(), inc=None):
        return p.op("scalar", fn, waits=waits, inc=inc)

    def SY(fn, waits=(), inc=None, n=16):
        w = list(waits)
        if inc is not None:
            w.append((inc, p.val[inc]))
        return p.op("sync", fn, waits=w, inc=inc, n=n)

    def GP(fn, waits=(), inc=None, n=16):
        w = list(waits)
        if inc is not None:
            w.append((inc, p.val[inc]))
        return p.op("gpsimd", fn, waits=w, inc=inc, n=n)

    SY(lambda e: e.dma_start(out=I_f[:], in_=idd), inc=s_x)
    v_bg = SY(lambda e: e.dma_start(out=bg_s[:], in_=bgd), inc=s_x)
    g_ib = GP(lambda e: e.dma_start(out=I_b[:], in_=idd), inc=s_w)
    g_wg = GP(lambda e: e.dma_start(out=wg_b[:], in_=wgd.rearrange("(c p) e -> p c e", p=128)), inc=s_w)
    g_b2 = GP(lambda e: e.dma_start(out=b2_b[:], in_=b2d), inc=s_w)
    v_ones = DV(lambda v: v.memset(ones_f[:], 1.0), inc=s_dv, drain=False)

    stk_b1 = ExitStack()
    b1_s = sb(stk_b1, "b1_s", [NEXP, DFF], F32)
    v_b1 = SY(lambda e: e.dma_start(out=b1_s[:], in_=b1d), inc=s_x)
    stk_pre = ExitStack()
    ps_pre = psum(stk_pre, "ps_pre", [128, 256])
    m_b1t = None
    for c in range(32):
        m_b1t = MM(ps_pre[:, 8 * c:8 * c + 8], b1_s[:, 128 * c:128 * (c + 1)], I_f[0:NEXP, 0:NEXP],
                   start=(c == 0), stop=(c == 31),
                   waits=([(s_x, v_b1)] if c == 0 else []),
                   inc=(s_mm if c == 31 else None))
    b1ev = DV(lambda v: v.tensor_copy(out=b1T[:], in_=ps_pre[:]),
              waits=[(s_mm, m_b1t)], inc=s_dv, drain=False)
    stk_pre.close()
    stk_b1.close()

    stk_A = ExitStack()
    q_b = sb(stk_A, "q_b", [128, 8, TPC], BF16)
    k_b = sb(stk_A, "k_b", [128, 8, XTOK], BF16)
    v_b = sb(stk_A, "v_b", [128, XCH, D], BF16)

    stk_B = ExitStack()
    xch_q = sb(stk_B, "xch_q", [128, 2, D], F32)
    xb_q = sb(stk_B, "xb_q", [128, 2, D], BF16)
    xT_S = sb(stk_B, "xT_S", [128, 2, 8, 512], BF16)
    wq_t = W1g.ap().rearrange("p c (b f) -> p b c f", b=2)
    wv_b = W2g
    ps_tr = [psum(stk_B, f"ps_tr{i}", [128, 1024]) for i in range(2)]
    bk_tr = [Bank(t[:], [(s_dv, b1ev)]) for t in ps_tr]
    ps_qk = [psum(stk_B, f"ps_qk{i}", [128, 512]) for i in range(2)]
    bk_qk = [Bank(t[:], [(s_dv, b1ev)]) for t in ps_qk]

    g_wv = GP(lambda e: e.dma_start(out=wv_b[:], in_=wqkv[:, 2 * D:3 * D].rearrange("(c p) f -> p c f", p=128)),
              waits=[(s_mm, m_b1t)], inc=s_w)

    slices = [(0, 512), (512, 1024), (1024, 1536), (1536, 2048), (2048, 2304)]
    trdone = {}
    cast_v = {}
    wq_used = [None, None]
    xt_used = [None, None]
    wq_i = 0
    grp = 0
    for si, (s0, s1) in enumerate(slices):
        LS = s1 - s0
        sp = si % 2
        jlist = list(range(s0 // 128, s1 // 128))
        xt_ready = None
        for j in jlist:
            bs = j % 2
            wl = [(s_mm, m_b1t)] + ([(s_mm, trdone[j - 2])] if (j - 2) in trdone else [])
            xj = SY((lambda _j, _b: lambda e: e.dma_start(out=xch_q[:, _b, :],
                                                          in_=xhd[128 * _j:128 * (_j + 1), :]))(j, bs),
                    waits=wl, inc=s_x)
            cw = [(s_x, xj)] + ([(s_mm, trdone[j - 2])] if (j - 2) in trdone else [])
            cast_v[j] = DV((lambda _b: lambda v: v.tensor_copy(out=xb_q[:, _b, :], in_=xch_q[:, _b, :]))(bs),
                           waits=cw, inc=s_dv, drain=False)
            bank = bk_tr[bs]
            for dc in range(8):
                half, q4 = dc // 4, dc % 4
                w_ = [(s_dv, cast_v[j]), (s_w, g_ib)] + (bank.rw() if dc == 0 else [])
                trdone[j] = MM(bank.ap[:, 512 * half + 128 * q4:512 * half + 128 * q4 + 128],
                               xb_q[:, bs, 128 * dc:128 * (dc + 1)], I_b[:],
                               start=(q4 == 0), stop=(q4 == 3), waits=w_,
                               inc=(s_mm if dc == 7 else None))
            jr = j * 128 - s0
            evw = [(s_mm, trdone[j])] + ([xt_used[sp]] if xt_used[sp] is not None else [])
            DV((lambda _jr, _bk, _sp: lambda v: v.tensor_copy(out=xT_S[:, _sp, 0:4, _jr:_jr + 128],
                                                              in_=_bk.ap[:, 0:512]))(jr, bank, sp),
               waits=evw, inc=s_dv, drain=False)
            xt_ready = DV((lambda _jr, _bk, _sp: lambda v: v.tensor_copy(out=xT_S[:, _sp, 4:8, _jr:_jr + 128],
                                                                         in_=_bk.ap[:, 512:1024]))(jr, bank, sp),
                          inc=s_dv, drain=False)
            bank.release = [(s_dv, xt_ready)]

        for kind in ("k", "q"):
            if kind == "q":
                c0 = max(s0, 128) - 128
                c1 = min(s1, 128 + TPC) - 128
                if c1 <= c0:
                    continue
                a0, NN = c0 + 128 - s0, c1 - c0
            else:
                a0, NN = 0, LS
            for op4 in range(2):
                col = (0 if kind == "q" else D) + 512 * op4
                wbuf = wq_i % 2
                wsem = s_w if wbuf == 0 else s_w2
                wl = [wq_used[wbuf]] if wq_used[wbuf] is not None else []
                wql = GP((lambda _c, _b: lambda e: e.dma_start(
                         out=wq_t[:, _b], in_=wqkv[:, _c:_c + 512].rearrange("(c p) f -> p c f", p=128)))(col, wbuf),
                         waits=wl, inc=wsem)
                wq_i += 1
                for sub in range(4):
                    oc = 4 * op4 + sub
                    bank = bk_qk[grp % 2]
                    grp += 1
                    gend = None
                    for dc in range(8):
                        w_ = [(wsem, wql), (s_dv, xt_ready)] + (bank.rw() if dc == 0 else [])
                        gend = MM(bank.ap[:, 0:NN], wq_t[:, wbuf, dc, 128 * sub:128 * sub + 128],
                                  xT_S[:, sp, dc, a0:a0 + NN],
                                  start=(dc == 0), stop=(dc == 7), waits=w_,
                                  inc=(s_mm if dc == 7 else None))
                    if kind == "q":
                        eva = AC((lambda _oc, _c0, _n, _bk: lambda s: s.mul(
                                 out=q_b[:, _oc, _c0:_c0 + _n], in_=_bk.ap[:, 0:_n], mul=QSCALE))(oc, c0, NN, bank),
                                 waits=[(s_mm, gend)], inc=s_ac)
                    else:
                        eva = AC((lambda _oc, _s0, _n, _bk: lambda s: s.copy(
                                 out=k_b[:, _oc, _s0:_s0 + _n], in_=_bk.ap[:, 0:_n]))(oc, s0, NN, bank),
                                 waits=[(s_mm, gend)], inc=s_ac)
                    bank.release = [(s_ac, eva)]
                wq_used[wbuf] = (s_mm, gend)

        for j in jlist:
            jr = j * 128 - s0
            for dh in range(2):
                bank = bk_qk[grp % 2]
                grp += 1
                gend = None
                for dc in range(8):
                    w_ = [(s_w, g_wv), (s_dv, xt_ready)] + (bank.rw() if dc == 0 else [])
                    gend = MM(bank.ap[:], xT_S[:, sp, dc, jr:jr + 128], wv_b[:, dc, 512 * dh:512 * (dh + 1)],
                              start=(dc == 0), stop=(dc == 7), waits=w_,
                              inc=(s_mm if dc == 7 else None))
                eva = AC((lambda _j, _dh, _bk: lambda s: s.copy(
                         out=v_b[:, _j, 512 * _dh:512 * (_dh + 1)], in_=_bk.ap[:]))(j, dh, bank),
                         waits=[(s_mm, gend)], inc=s_ac)
                bank.release = [(s_ac, eva)]
        xt_used[sp] = (s_mm, gend)

    qkv_mm_end = p.val[s_mm]
    qkv_ac_end = p.val[s_ac]
    qkv_dv_end = p.val[s_dv]

    def w1_ap(u):
        e, qf = u // 4, u % 4
        return w1d[e, :, FQ * qf:FQ * (qf + 1)].rearrange("(c p) f -> p c f", p=128)

    def w2_ap(u):
        e, qf = u // 4, u % 4
        return w2d[e, FQ * qf:FQ * (qf + 1), :].rearrange("(c p) f -> p c f", p=128)

    wld_w1_0 = GP(lambda e: e.dma_start(out=W1g[:], in_=w1_ap(0)),
                  waits=[(s_mm, qkv_mm_end)], inc=s_w)
    wld_w2_0 = GP(lambda e: e.dma_start(out=W2g[:], in_=w2_ap(0)), inc=s_w2)
    stk_B.close()

    stk_C = ExitStack()
    xch = sb(stk_C, "xch", [128, 2, D], F32)
    xpre = sb(stk_C, "xpre", [128, 2, D], F32)
    attn_sb = sb(stk_C, "attn_sb", [128, 2, D], BF16)
    x1tmp = sb(stk_C, "x1tmp", [128, 2, D], BF16)
    probs_u = sb(stk_C, "probs_u", [128, 3, 384], BF16)
    probsT = sb(stk_C, "probsT", [128, 3, 384], BF16)
    mask_s0 = sb(stk_C, "mask_s0", [128, 384], F32)
    sq_scr = sb(stk_C, "sq_scr", [128, D], F32)
    c_init = [(s_dv, qkv_dv_end), (s_ac, qkv_ac_end)]
    ps_sc = [psum(stk_C, f"ps_sc{i}", [128, 384]) for i in range(2)]
    bk_sc = [Bank(t[:], c_init) for t in ps_sc]
    ps_tm = psum(stk_C, "ps_tm", [128, 512])
    bk_tm = Bank(ps_tm[:], c_init)
    ps_pt = [psum(stk_C, f"ps_pt{i}", [128, 512]) for i in range(2)]
    bk_pt = [Bank(t[:], c_init) for t in ps_pt]
    ps_av = [psum(stk_C, f"ps_av{i}", [128, 512]) for i in range(3)]
    bk_av = [Bank(t[:], c_init) for t in ps_av]

    def avbank(w, h):
        return (4 * (w // 2) + (2 if h >= 8 else 0) + (w % 2)) % 3

    v_mr = SY(lambda e: e.dma_start(out=mask_s0[:], in_=mld),
              waits=[(s_mm, qkv_mm_end), (s_ac, qkv_ac_end)], inc=s_x)
    mask_state = {"v": v_mr, "reload": None}
    xwld = {
        0: SY(lambda e: e.dma_start(out=xch[:, 0, :], in_=xhd[128:256, :]), inc=s_x),
        1: SY(lambda e: e.dma_start(out=xch[:, 1, :], in_=xhd[256:384, :]), inc=s_x),
    }

    steps = []
    for ipair in range(0, NCH, 2):
        for h in range(H):
            steps.append((ipair, h))
            steps.append((ipair + 1, h))
    NST = len(steps)

    sc_v, exp_v, tr_v, av_v, recip_v, evac_v, evpt_v = {}, {}, {}, {}, {}, {}, {}
    stt_v, sq_v, x1tmm_v = {}, {}, {}
    state = {"ug": None, "wtail": None, "dbg_xpre": {}, "dbg_u": None}
    scb = {}

    def lri(w, h):
        return (w % 2) * 16 + h

    def plan_scores(g):
        w, h = steps[g]
        bi = g % 2
        scb[g] = bi
        hp, hb = h // 2, (h % 2) * 64
        w_ = list(bk_sc[bi].rw())
        if g < 2:
            w_ += [(s_ac, qkv_ac_end)]
        sc_v[g] = MM(ps_sc[bi][:],
                     q_b[hb:hb + 64, hp, 128 * w:128 * (w + 1)],
                     k_b[hb:hb + 64, hp, 128 * w:128 * w + 384],
                     start=True, stop=True, waits=w_, inc=s_mm)

    def st_nm_exp(g):
        w, h = steps[g]
        bi = scb[g]
        bp = g % 3
        if w in (0, NCH - 1):
            mv = DV((lambda _b: lambda v: v.tensor_tensor(
                out=ps_sc[_b][:], in0=ps_sc[_b][:], in1=mask_s0[:], op=ALU.add))(bi),
                waits=[(s_mm, sc_v[g]), (s_x, mask_state["v"])], inc=s_dv, drain=True)
            if w == 0 and h == H - 1:
                mask_state["reload"] = mv
        nmw = [(s_mm, sc_v[g])]
        if g - 3 in exp_v:
            nmw.append((s_ac, exp_v[g - 3]))
        nmv = DV((lambda _bi, _bp: lambda v: v.reduce_max(out=stc(C_NM + _bp), in_=ps_sc[_bi][:],
                                                          axis=AX.X, negate=True))(bi, bp),
                 waits=nmw, inc=s_dv, drain=False)
        ew = [(s_dv, nmv)]
        if g - 3 in tr_v:
            ew.append((s_mm, tr_v[g - 3]))
        if (w - 2, h) in recip_v:
            ew.append((s_dv, recip_v[(w - 2, h)]))
        exp_v[g] = AC((lambda _bi, _bp, _s: lambda s: s.activation(
            out=probs_u[:, _bp, :], in_=ps_sc[_bi][:], func=AF.Exp,
            bias=stc(C_NM + _bp), scale=1.0, accum_out=stc(C_L + _s)))(bi, bp, lri(w, h)),
            waits=ew, inc=s_ac)
        bk_sc[bi].release = [(s_ac, exp_v[g])]

    def st_tr(g):
        bp = g % 3
        bt = g % 2
        for c in range(3):
            w_ = [(s_ac, exp_v[g])] + (bk_pt[bt].rw() if c == 0 else [])
            tr_v[g] = MM(ps_pt[bt][:, 128 * c:128 * (c + 1)],
                         probs_u[:, bp, 128 * c:128 * (c + 1)], I_b[:],
                         start=(c == 0), stop=(c == 2), waits=w_,
                         inc=(s_mm if c == 2 else None))

    def st_evpt(g):
        bt = g % 2
        sl = g % 3
        epw = [(s_mm, tr_v[g])]
        if g - 3 in av_v:
            epw.append((s_mm, av_v[g - 3]))
        if g % 2 == 1:
            ev = AC((lambda _b, _s: lambda s: s.copy(out=probsT[:, _s, :], in_=ps_pt[_b][:, 0:384]))(bt, sl),
                    waits=epw, inc=s_ac)
            bk_pt[bt].release = [(s_ac, ev)]
            evpt_v[g] = ("ac", ev)
        else:
            ev = DV((lambda _b, _s: lambda v: v.tensor_copy(out=probsT[:, _s, :], in_=ps_pt[_b][:, 0:384]))(bt, sl),
                    waits=epw, inc=s_dv, drain=False)
            bk_pt[bt].release = [(s_dv, ev)]
            evpt_v[g] = ("dv", ev)

    def st_av(g):
        w, h = steps[g]
        sl = g % 3
        aw = avbank(w, h)
        kind, ev = evpt_v[g]
        evw = (s_ac, ev) if kind == "ac" else (s_dv, ev)
        if h % 8 == 0:
            ug = 4 * (w // 2) + (2 if h >= 8 else 0) + (w % 2) - 3
            if ug >= 0:
                tw = 2 * (ug // 4) + (ug % 2)
                th = 7 if (ug % 4) < 2 else 15
                force_drain(lambda: (tw, th) in evac_v)
        for c in range(3):
            w_ = [evw]
            if h % 8 == 0 and c == 0:
                w_ += bk_av[aw].rw()
            av_v[g] = MM(ps_av[aw][:, 64 * (h % 8):64 * (h % 8) + 64],
                         probsT[:, sl, 128 * c:128 * (c + 1)], v_b[:, w + c, 64 * h:64 * h + 64],
                         start=(h % 8 == 0 and c == 0), stop=(h % 8 == 7 and c == 2),
                         waits=w_, inc=(s_mm if c == 2 else None))
        avg_by_wh[(w, h)] = av_v[g]

    def st_recip(g):
        w, h = steps[g]
        rw_ = [(s_ac, exp_v[g])]
        if (w - 2, h) in evac_v:
            rw_.append((s_dv, evac_v[(w - 2, h)]))
        recip_v[(w, h)] = DV((lambda _s: lambda v: v.reciprocal(
            out=stc(C_RI + _s), in_=stc(C_L + _s)))(lri(w, h)),
            waits=rw_, inc=s_dv, drain=False)

    def plan_evac(w, h):
        aw = avbank(w, h)
        gate = avg_by_wh[(w, 7)] if h < 8 else avg_by_wh[(w, 15)]
        w_ = [(s_mm, gate), (s_dv, recip_v[(w, h)])]
        if h == 0 and (w - 2) in stt_v:
            w_.append((s_dv, stt_v[w - 2]))
        evac_v[(w, h)] = DV((lambda _w, _h, _aw, _s: lambda v: v.tensor_scalar(
            out=attn_sb[:, _w % 2, 64 * _h:64 * _h + 64],
            in0=ps_av[_aw][:, 64 * (_h % 8):64 * (_h % 8) + 64],
            scalar1=stc(C_RI + _s), scalar2=None, op0=ALU.mult))(w, h, aw, lri(w, h)),
            waits=w_, inc=s_dv, drain=False)
        if h % 8 == 7:
            bk_av[aw].release = [(s_dv, evac_v[(w, h)])]

    def plan_stt(w):
        par = w % 2
        w_ = [(s_dv, evac_v[(w, 15)]), (s_x, xwld[w])]
        if (w - 2) in sq_v:
            w_.append((s_ac, sq_v[w - 2]))
        if (w - 2) in state["dbg_xpre"]:
            w_.append((s_x, state["dbg_xpre"][w - 2]))
        stt_v[w] = DV((lambda _p: lambda v: v.scalar_tensor_tensor(
            out=xpre[:, _p, :], in0=attn_sb[:, _p, :], scalar=1.0, in1=xch[:, _p, :],
            op0=ALU.mult, op1=ALU.add, accum_out=stc(C_S1 + _p)))(par),
            waits=w_, inc=s_dv)
        if w + 2 < NCH:
            xwld[w + 2] = SY((lambda _i: lambda e: e.dma_start(
                out=xch[:, _i % 2, :], in_=xhd[128 * (_i + 1):128 * (_i + 2), :]))(w + 2),
                waits=[(s_dv, stt_v[w])], inc=s_x)
        if debug:
            state["dbg_xpre"][w] = SY((lambda _i, _p: lambda e: e.dma_start(
                out=d_xpre[128 * _i:128 * (_i + 1), :], in_=xpre[:, _p, :]))(w, par),
                waits=[(s_dv, stt_v[w])], inc=s_x)

    def plan_tail_stats1(j):
        par = j % 2
        xp = xpre[:, par, :]
        w_ = [(s_dv, stt_v[j])]
        if (j - 1) in sq_v:
            w_.append((s_ac, sq_v[j - 1]))
        sq_v[j] = AC((lambda _xp, _p: lambda s: s.activation(
            out=sq_scr[:], in_=_xp, func=AF.Square, accum_out=stc(C_S2 + _p)))(xp, par),
            waits=w_, inc=s_ac)
        DV((lambda _p: lambda v: v.tensor_scalar(out=stc(C_M + _p), in0=stc(C_S1 + _p), scalar1=1.0 / D,
                                                 scalar2=None, op0=ALU.mult))(par),
           waits=[(s_dv, stt_v[j])], drain=False)

    def plan_tail_stats2(j):
        par = j % 2
        DV((lambda _p: lambda v: v.tensor_scalar(out=stc(C_EX2 + _p), in0=stc(C_S2 + _p), scalar1=1.0 / D,
                                                 scalar2=None, op0=ALU.mult))(par),
           waits=[(s_ac, sq_v[j])])
        DV((lambda _p: lambda v: v.scalar_tensor_tensor(out=stc(C_VAR + _p), in0=stc(C_M + _p),
                                                        scalar=stc(C_M + _p), in1=stc(C_EX2 + _p),
                                                        op0=ALU.mult, op1=ALU.subtract))(par))
        vpe = DV((lambda _p: lambda v: v.tensor_scalar(out=stc(C_VPE + _p), in0=stc(C_VAR + _p),
                                                       scalar1=-1.0, scalar2=EPS,
                                                       op0=ALU.mult, op1=ALU.add))(par),
                 inc=s_dv, drain=False)
        lnv = AC((lambda _p: lambda s: s.activation(out=stc(C_SSTD + _p), in_=stc(C_VPE + _p),
                                                    func=AF.Ln))(par),
                 waits=[(s_dv, vpe)], inc=s_ac)
        tail_rstd[j] = AC((lambda _p: lambda s: s.activation(out=stc(C_RSTD + _p), in_=stc(C_SSTD + _p),
                                                             func=AF.Exp, scale=-0.5))(par),
                          waits=[(s_ac, lnv)], inc=s_ac)

    def plan_tail_stats3(j):
        par = j % 2
        DV((lambda _p: lambda v: v.tensor_scalar(out=stc(C_NMB + _p), in0=stc(C_M + _p),
                                                 scalar1=stc(C_RSTD + _p), scalar2=-1.0,
                                                 op0=ALU.mult, op1=ALU.mult))(par),
           waits=[(s_ac, tail_rstd[j])])
        xp = xpre[:, par, :]
        apw = [(s_mm, x1tmm_v[j - 2])] if (j - 2) in x1tmm_v else []
        if debug and (j - 2) in dbg_x1_v:
            apw.append((s_w, dbg_x1_v[j - 2]))
        x1ap_v[j] = DV((lambda _j, _xp, _p: lambda v: v.tensor_scalar(
            out=x1tmp[:, _j % 2, :], in0=_xp, scalar1=stc(C_RSTD + _p), scalar2=stc(C_NMB + _p),
            op0=ALU.mult, op1=ALU.add))(j, xp, par), waits=apw, inc=s_dv)
        if debug:
            dbg_x1_v[j] = GP((lambda _j: lambda e: e.dma_start(out=d_x1[128 * _j:128 * (_j + 1), :],
                                                               in_=x1tmp[:, _j % 2, :]))(j),
                             waits=[(s_dv, x1ap_v[j])], inc=s_w)

    def gen_tail_mm_pieces(j):
        tm = {}

        def p1():
            mm = None
            for q4 in range(4):
                w_ = [(s_dv, x1ap_v[j])]
                if q4 == 0:
                    w_ += bk_tm.rw()
                    if state["wtail"] is not None:
                        w_.append((s_dv, state["wtail"]))
                    if state["ug"] is not None:
                        w_.append((s_ac, state["ug"]))
                    if debug:
                        w_.append((s_w, p.val[s_w]))
                mm = MM(ps_tm[:, 128 * q4:128 * q4 + 128],
                        x1tmp[:, j % 2, 128 * q4:128 * (q4 + 1)], I_b[:],
                        start=(q4 == 0), stop=(q4 == 3), waits=w_,
                        inc=(s_mm if q4 == 3 else None))
            tm["mm1"] = mm

        def p2():
            tm["cast1"] = DV((lambda _j: lambda v: v.tensor_copy(
                out=x1T[:, 0:4, 128 * _j:128 * (_j + 1)], in_=ps_tm[:, 0:512]))(j),
                waits=[(s_mm, tm["mm1"])], inc=s_dv, drain=False)

        def p3():
            mm = None
            for q4 in range(4):
                w_ = [(s_dv, x1ap_v[j])]
                if q4 == 0:
                    w_.append((s_dv, tm["cast1"]))
                mm = MM(ps_tm[:, 128 * q4:128 * q4 + 128],
                        x1tmp[:, j % 2, 128 * (4 + q4):128 * (5 + q4)], I_b[:],
                        start=(q4 == 0), stop=(q4 == 3), waits=w_,
                        inc=(s_mm if q4 == 3 else None))
            tm["mm2"] = mm
            x1tmm_v[j] = mm

        def p4():
            tm["cast2"] = DV((lambda _j: lambda v: v.tensor_copy(
                out=x1T[:, 4:8, 128 * _j:128 * (_j + 1)], in_=ps_tm[:, 0:512]))(j),
                waits=[(s_mm, tm["mm2"])], inc=s_dv, drain=False)

        def p5():
            MM(ps_tm[:, 0:8], ones_f[:], bg_s[:], start=True, stop=False,
               waits=[(s_dv, tm["cast2"]), (s_x, v_bg), (s_dv, v_ones)])
            gmm = None
            for dc in range(8):
                gmm = MM(ps_tm[:, 0:8], x1T[:, dc, 128 * j:128 * (j + 1)], wg_b[:, dc, :],
                         start=False, stop=(dc == 7),
                         waits=[(s_w, g_wg)], inc=(s_mm if dc == 7 else None))
            tm["gmm"] = gmm

        def p6():
            ngw = [(s_mm, tm["gmm"])]
            if state["ug"] is not None:
                ngw.append((s_ac, state["ug"]))
            nmg = DV(lambda v: v.reduce_max(out=stc(C_NMG), in_=ps_tm[:, 0:8],
                                            axis=AX.X, negate=True),
                     waits=ngw, inc=s_dv, drain=False)
            state["ug"] = AC(lambda s: s.activation(out=U8, in_=ps_tm[:, 0:8],
                                                    func=AF.Exp, bias=stc(C_NMG)),
                             waits=[(s_dv, nmg)] + ([(s_x, state["dbg_u"])] if state["dbg_u"] is not None else []),
                             inc=s_ac)
            bk_tm.release = [(s_dv, nmg), (s_ac, state["ug"])]

        def p7():
            DV(lambda v: v.reduce_max(out=stc(C_M2G), in_=U8, axis=AX.X), waits=[(s_ac, state["ug"])])
            DV(lambda v: v.tensor_scalar(out=G8A, in0=U8, scalar1=stc(C_M2G), scalar2=None, op0=ALU.is_ge))
            DV(lambda v: v.scalar_tensor_tensor(out=G8B, in0=G8A, scalar=-1e30, in1=U8,
                                                op0=ALU.mult, op1=ALU.add))

        def p8():
            DV(lambda v: v.reduce_max(out=stc(C_M2G), in_=G8B, axis=AX.X))
            DV(lambda v: v.tensor_scalar(out=G8A, in0=U8, scalar1=stc(C_M2G), scalar2=None, op0=ALU.is_ge))

        def p9():
            DV(lambda v: v.scalar_tensor_tensor(out=G8B, in0=U8, scalar=1.0, in1=G8A,
                                                op0=ALU.mult, op1=ALU.mult, accum_out=stc(C_SSG)))
            DV(lambda v: v.reciprocal(out=stc(C_RIG), in_=stc(C_SSG)))
            state["wtail"] = DV((lambda _j: lambda v: v.tensor_scalar(
                out=wgt[:, _j, :], in0=G8B, scalar1=stc(C_RIG), scalar2=None, op0=ALU.mult))(j),
                inc=s_dv)
            if debug:
                state["dbg_u"] = SY((lambda _j: lambda e: e.dma_start(
                    out=d_u[128 * _j:128 * (_j + 1), :], in_=U8))(j),
                    waits=[(s_dv, state["wtail"])], inc=s_x)

        return [p1, p2, p3, p4, p5, p6, p7, p8, p9]

    def plan_tail_mm(j, bt=None):
        for pc in gen_tail_mm_pieces(j):
            pc()

    avg_by_wh = {}
    x1ap_v = {}
    dbg_x1_v = {}
    tail_rstd = {}
    pend_evac = []
    pend_tailmm = []
    pend_tmpiece = []
    pend_work = []

    def force_drain(cond):
        while not cond():
            we, he = pend_evac.pop(0)
            plan_evac(we, he)
            if he == 15:
                plan_stt(we)
                plan_tail_stats1(we)
                pend_work.append([2, (lambda _w: lambda: plan_tail_stats2(_w))(we)])
                pend_work.append([4, (lambda _w: lambda: plan_tail_stats3(_w))(we)])

    plan_scores(0)
    plan_scores(1)
    for g in range(NST):
        w, h = steps[g]
        st_nm_exp(g)
        if g + 2 < NST:
            plan_scores(g + 2)
        st_tr(g)
        if g >= 2:
            st_recip(g - 2)
        if g >= 1:
            st_evpt(g - 1)
            st_av(g - 1)
            wp, hp_ = steps[g - 1]
            if hp_ == 7:
                pend_evac += [(wp, hh) for hh in range(8)]
            if hp_ == 15:
                pend_evac += [(wp, hh) for hh in range(8, 16)]
        if mask_state["reload"] is not None and not mask_state.get("reloaded"):
            mask_state["v"] = SY(lambda e: e.dma_start(out=mask_s0[:], in_=mrd),
                                 waits=[(s_dv, mask_state["reload"])], inc=s_x)
            mask_state["reloaded"] = True
        n_done = 0
        while pend_evac and n_done < 3:
            we, he = pend_evac[0]
            if (we, he) not in recip_v:
                break
            plan_evac(we, he)
            pend_evac.pop(0)
            n_done += 1
            if he == 15:
                plan_stt(we)
                plan_tail_stats1(we)
                pend_work.append([2, (lambda _w: lambda: plan_tail_stats2(_w))(we)])
                pend_work.append([4, (lambda _w: lambda: plan_tail_stats3(_w))(we)])
        for item in pend_work:
            item[0] -= 1
        while pend_work and pend_work[0][0] <= 0:
            pend_work.pop(0)[1]()
        if h == 0 and w % 2 == 0 and w >= 2:
            pend_tailmm += [w - 2, w - 1]
        while pend_tailmm and pend_tailmm[0] in x1ap_v:
            pend_tmpiece += gen_tail_mm_pieces(pend_tailmm.pop(0))
        if pend_tmpiece:
            pend_tmpiece.pop(0)()
    st_evpt(NST - 1)
    st_av(NST - 1)
    wlast, hlast = steps[NST - 1]
    pend_evac += [(wlast, hh) for hh in range(8, 16)]
    for g in range(NST - 2, NST):
        if steps[g] not in recip_v:
            st_recip(g)
    while pend_evac:
        we, he = pend_evac.pop(0)
        plan_evac(we, he)
        if he == 15:
            plan_stt(we)
            plan_tail_stats1(we)
            plan_tail_stats2(we)
            plan_tail_stats3(we)
    while pend_work:
        pend_work.pop(0)[1]()
    while pend_tmpiece:
        pend_tmpiece.pop(0)()
    pend_tailmm += [NCH - 2, NCH - 1]
    while pend_tailmm:
        plan_tail_mm(pend_tailmm.pop(0))

    win_mm_end = p.val[s_mm]
    win_dv_end = p.val[s_dv]
    win_ac_end = p.val[s_ac]
    stk_C.close()

    stk_A.close()
    stk_D = ExitStack()
    wT_b = sb(stk_D, "wT_b", [NEXP, TPC], BF16)
    y_acc = sb(stk_D, "y_acc", [128, NCH, D], F32)
    hpr = sb(stk_D, "hpr", [128, 8, TPC], BF16)
    outb = sb(stk_D, "outb", [128, 2, D], F32)
    wb8 = sb(stk_D, "wb8", [128, NEXP], BF16)
    win_dv_late = max(evac_v[(NCH - 1, 15)], x1ap_v[NCH - 1])
    d_init = [(s_dv, win_dv_late), (s_ac, win_ac_end)]
    ps_h = [psum(stk_D, f"ps_h{i}", [128, 1024]) for i in range(2)]
    bk_h = [Bank(t[:], d_init) for t in ps_h]
    ps_y = [psum(stk_D, f"ps_y{i}", [128, 1024]) for i in range(2)]
    bk_y = [Bank(t[:], d_init) for t in ps_y]

    alias_w = [(s_mm, win_mm_end), (s_dv, win_dv_late), (s_ac, win_ac_end), (s_x, p.val[s_x])]

    wld = {}
    wld["w1", 0] = wld_w1_0
    wld["w2", 0] = wld_w2_0

    wtmm = None
    prev_cp = None
    for i in range(NCH):
        g = i // 8
        cw = alias_w + ([(s_mm, wtmm)] if wtmm is not None else [])
        cp = DV((lambda _i: lambda v: v.tensor_copy(out=wb8[:], in_=wgt[:, _i, :]))(i),
                waits=cw, inc=s_dv, drain=False)
        w_ = [(s_dv, cp)] + bk_y[g].rw()
        wtmm = MM(ps_y[g][0:NEXP, 128 * (i % 8):128 * (i % 8) + 128], wb8[:], I_b[:],
                  start=(i % 4 == 0), stop=(i % 4 == 3), waits=w_, inc=s_mm)
    wt_ev = None
    for g in range(2):
        wt_ev = DV((lambda _g: lambda v: v.tensor_copy(out=wT_b[:, 1024 * _g:1024 * (_g + 1)],
                                                       in_=ps_y[_g][0:NEXP, :]))(g),
                   waits=[(s_mm, wtmm)], inc=s_dv, drain=False)

    for i in range(NCH):
        pp = i % 2
        gend = None
        for dh in range(2):
            w_ = [(s_dv, wt_ev), (s_w, g_b2)] + (bk_y[pp].rw() if dh == 0 else [])
            gend = MM(ps_y[pp][:, 512 * dh:512 * (dh + 1)], wT_b[:, 128 * i:128 * (i + 1)],
                      b2_b[:, 512 * dh:512 * (dh + 1)],
                      start=True, stop=True, waits=w_, inc=(s_mm if dh == 1 else None))
        ev = DV((lambda _i, _pp: lambda v: v.tensor_copy(out=y_acc[:, _i, :], in_=ps_y[_pp][:]))(i, pp),
                waits=[(s_mm, gend)], inc=s_dv)
        bk_y[pp].release = [(s_dv, ev)]

    tbank = [bk_h[0], bk_y[0]]
    fbank = [bk_h[1], bk_y[1]]
    fin_v, sqf_v, ap_v, st_v, tbk_v = {}, {}, {}, {}, {}
    fpair = {i: (0 if i <= 8 else (1 if (i - 9) % 2 == 0 else 0)) for i in range(NCH)}
    prevu = {}

    def plan_trans_fin(i):
        s = fpair[i]
        pv = prevu.get(s)
        tbk = None
        for dc in range(8):
            half, q4 = dc // 4, dc % 4
            w_ = []
            if dc == 0:
                w_ = tbank[s].rw()
                if pv in sqf_v:
                    w_.append((s_ac, sqf_v[pv]))
            tbk = MM(tbank[s].ap[:, 512 * half + 128 * q4:512 * half + 128 * q4 + 128],
                     x1T[:, dc, 128 * i:128 * (i + 1)], I_b[:],
                     start=(q4 == 0), stop=(q4 == 3), waits=w_,
                     inc=(s_mm if dc == 7 else None))
        tbk_v[i] = tbk
        w_ = [(s_mm, tbk), (s_dv, yev31[i])] + fbank[s].rw()
        if pv in ap_v:
            w_.append((s_dv, ap_v[pv]))
        if pv in sqf_v:
            w_.append((s_ac, sqf_v[pv]))
        fin_v[i] = DV((lambda _i, _s: lambda v: v.scalar_tensor_tensor(
            out=fbank[_s].ap[:], in0=tbank[_s].ap[:], scalar=1.0, in1=y_acc[:, _i, :],
            op0=ALU.mult, op1=ALU.add, accum_out=stc(C_S1 + _i % 2)))(i, s),
            waits=w_, inc=s_dv)
        tbank[s].release = [(s_dv, fin_v[i])]

    def plan_fintail(j):
        s = fpair[j]
        sqf_v[j] = AC((lambda _s, _p: lambda sc_: sc_.activation(
            out=tbank[_s].ap[:], in_=fbank[_s].ap[:], func=AF.Square,
            accum_out=stc(C_S2 + _p)))(s, j % 2),
            waits=[(s_dv, fin_v[j])], inc=s_ac)
        par = j % 2
        DV((lambda _p: lambda v: v.tensor_scalar(out=stc(C_M + _p), in0=stc(C_S1 + _p), scalar1=1.0 / D,
                                                 scalar2=None, op0=ALU.mult))(par),
           waits=[(s_dv, fin_v[j])], drain=False)
        DV((lambda _p: lambda v: v.tensor_scalar(out=stc(C_EX2 + _p), in0=stc(C_S2 + _p), scalar1=1.0 / D,
                                                 scalar2=None, op0=ALU.mult))(par),
           waits=[(s_ac, sqf_v[j])])
        DV((lambda _p: lambda v: v.scalar_tensor_tensor(out=stc(C_VAR + _p), in0=stc(C_M + _p),
                                                        scalar=stc(C_M + _p), in1=stc(C_EX2 + _p),
                                                        op0=ALU.mult, op1=ALU.subtract))(par))
        vpe = DV((lambda _p: lambda v: v.tensor_scalar(out=stc(C_VPE + _p), in0=stc(C_VAR + _p),
                                                       scalar1=-1.0, scalar2=EPS,
                                                       op0=ALU.mult, op1=ALU.add))(par),
                 inc=s_dv, drain=False)
        lnv = AC((lambda _p: lambda sc_: sc_.activation(out=stc(C_SSTD + _p), in_=stc(C_VPE + _p),
                                                        func=AF.Ln))(par),
                 waits=[(s_dv, vpe)], inc=s_ac)
        rstd_ac = AC((lambda _p: lambda sc_: sc_.activation(out=stc(C_RSTD + _p), in_=stc(C_SSTD + _p),
                                                            func=AF.Exp, scale=-0.5))(par),
                     waits=[(s_ac, lnv)], inc=s_ac)
        DV((lambda _p: lambda v: v.tensor_scalar(out=stc(C_NMB + _p), in0=stc(C_M + _p),
                                                 scalar1=stc(C_RSTD + _p), scalar2=-1.0,
                                                 op0=ALU.mult, op1=ALU.mult))(par),
           waits=[(s_ac, rstd_ac)])
        w_ = [(s_st, st_v[j - 2])] if (j - 2) in st_v else []
        ap_v[j] = DV((lambda _s, _j, _p: lambda v: v.tensor_scalar(
            out=outb[:, _j % 2, :], in0=fbank[_s].ap[:], scalar1=stc(C_RSTD + _p),
            scalar2=stc(C_NMB + _p), op0=ALU.mult, op1=ALU.add))(s, j, par),
            waits=w_, inc=s_dv)
        fbank[s].release = [(s_dv, ap_v[j]), (s_ac, sqf_v[j])]
        st_v[j] = SY((lambda _j: lambda e: e.dma_start(out=outd[128 * _j:128 * (_j + 1), :],
                                                       in_=outb[:, _j % 2, :]))(j),
                     waits=[(s_dv, ap_v[j])], inc=s_st)
        prevu[s] = j


    yev31 = {}
    for u in range(NUNIT):
        e, qf = u // 4, u % 4
        gel_pr = {}
        hend = None
        for pr in range(2):
            for fc in range(8):
                bank = bk_h[(pr * 8 + fc) % 2]
                gend = None
                for tt in range(2):
                    tq5 = pr * 2 + tt
                    for dc in range(8):
                        w_ = [(s_w, wld["w1", u])]
                        if tt == 0 and dc == 0:
                            w_ += bank.rw()
                        gend = MM(bank.ap[:, 512 * tt:512 * (tt + 1)],
                                  W1g[:, dc, 128 * fc:128 * (fc + 1)],
                                  x1T[:, dc, 512 * tq5:512 * (tq5 + 1)],
                                  start=(dc == 0), stop=(dc == 7), waits=w_,
                                  inc=(s_mm if (tt == 1 and dc == 7) else None))
                hend = gend
                col = 8 * (qf * 8 + fc) + e
                gel_pr[pr] = AC((lambda _fc, _pr, _bk, _col: lambda s: s.activation(
                    out=hpr[:, _fc, 1024 * _pr:1024 * (_pr + 1)], in_=_bk.ap[:],
                    func=AF.Gelu, bias=b1T[:, _col:_col + 1]))(fc, pr, bank, col),
                    waits=[(s_mm, gend)], inc=s_ac)
                bank.release = [(s_ac, gel_pr[pr])]
        wld["w1", u + 1] = GP((lambda _u: lambda e_: e_.dma_start(out=W1g[:], in_=w1_ap(_u)))(u + 1),
                              waits=[(s_mm, hend)], inc=s_w) if u + 1 < NUNIT else None
        yend = None
        for tq in range(NCH):
            pp = tq % 2
            bank = bk_y[pp]
            gend = None
            for dh in range(2):
                for fc in range(8):
                    w_ = [(s_w2, wld["w2", u]), (s_ac, gel_pr[tq // 8])]
                    if dh == 0 and fc == 0:
                        w_ += bank.rw()
                    gend = MM(bank.ap[:, 512 * dh:512 * (dh + 1)],
                              hpr[:, fc, 128 * tq:128 * (tq + 1)],
                              W2g[:, fc, 512 * dh:512 * (dh + 1)],
                              start=(fc == 0), stop=(fc == 7), waits=w_,
                              inc=(s_mm if (dh == 1 and fc == 7) else None))
            yend = gend
            ev = DV((lambda _tq, _pp, _e: lambda v: v.scalar_tensor_tensor(
                out=y_acc[:, _tq, :], in0=ps_y[_pp][:], scalar=wgt[:, _tq, _e:_e + 1],
                in1=y_acc[:, _tq, :], op0=ALU.mult, op1=ALU.add))(tq, pp, e),
                waits=[(s_mm, gend)], inc=s_dv)
            bk_y[pp].release = [(s_dv, ev)]
            if u == NUNIT - 1:
                yev31[tq] = ev
                if tq <= 8:
                    if tq >= 1:
                        plan_fintail(tq - 1)
                    plan_trans_fin(tq)
        if u + 1 < NUNIT:
            wld["w2", u + 1] = GP((lambda _u: lambda e_: e_.dma_start(out=W2g[:], in_=w2_ap(_u)))(u + 1),
                                  waits=[(s_mm, yend)], inc=s_w2)

    y_done = p.val[s_dv]
    if debug:
        for i in range(NCH):
            SY((lambda _i: lambda e: e.dma_start(out=d_y[128 * _i:128 * (_i + 1), :], in_=y_acc[:, _i, :]))(i),
               waits=[(s_dv, y_done)], inc=s_x)
            SY((lambda _i: lambda e: e.dma_start(out=d_w[128 * _i:128 * (_i + 1), :], in_=wgt[:, _i, :]))(i),
               inc=s_x)

    for i in range(9, NCH):
        plan_trans_fin(i)
        plan_fintail(i - 1)
    plan_fintail(NCH - 1)

    final_st = p.val[s_st]
    final_x = p.val[s_x]
    SY(lambda e: e.nop(), waits=[(s_st, final_st), (s_x, final_x)], inc=None)
    if debug:
        GP(lambda e: e.nop(), waits=[(s_w, p.val[s_w])], inc=None)

    stk_D.close()
    with nc.Block() as block:
        p.emit(block)
    P.close()
    return nc


def make_in_maps(L0_attn_x, L0_attn_w_qkv, L0_ffn_Wg, L0_ffn_bg, L0_ffn_W1,
                 L0_ffn_b1, L0_ffn_W2, L0_ffn_b2):
    x = np.ascontiguousarray(np.asarray(L0_attn_x, dtype=np.float32))
    wqkv = np.ascontiguousarray(np.asarray(L0_attn_w_qkv, dtype=np.float32))
    wg = np.ascontiguousarray(np.asarray(L0_ffn_Wg, dtype=np.float32))
    bg = np.ascontiguousarray(np.asarray(L0_ffn_bg, dtype=np.float32)).reshape(1, NEXP)
    w1 = np.ascontiguousarray(np.asarray(L0_ffn_W1, dtype=np.float32))
    b1 = np.ascontiguousarray(np.asarray(L0_ffn_b1, dtype=np.float32))
    w2 = np.ascontiguousarray(np.asarray(L0_ffn_W2, dtype=np.float32))
    b2 = np.ascontiguousarray(np.asarray(L0_ffn_b2, dtype=np.float32))
    ident = np.eye(128, dtype=np.float32)
    in_maps = []
    for c in range(8):
        seq, ch = c // 4, c % 4
        t0 = ch * TPC
        xh = np.zeros((XTOK, D), np.float32)
        lo, hi = t0 - W, t0 + TPC + W
        slo, shi = max(lo, 0), min(hi, L)
        xh[slo - lo: shi - lo] = x[seq, slo:shi]
        maskl = np.zeros((128, 384), np.float32)
        maskr = np.zeros((128, 384), np.float32)
        if ch == 0:
            maskl[:, :128] = -1e9
        if ch == 3:
            maskr[:, 256:] = -1e9
        in_maps.append(dict(xh=xh, wqkv=wqkv, wg=wg, bg=bg, b1=b1, b2=b2,
                            w1=w1, w2=w2, maskl=maskl, maskr=maskr, ident=ident))
    return in_maps


def kernel(L0_attn_x, L0_attn_w_qkv, L0_ffn_Wg, L0_ffn_bg, L0_ffn_W1,
           L0_ffn_b1, L0_ffn_W2, L0_ffn_b2):
    in_maps = make_in_maps(L0_attn_x, L0_attn_w_qkv, L0_ffn_Wg, L0_ffn_bg,
                           L0_ffn_W1, L0_ffn_b1, L0_ffn_W2, L0_ffn_b2)
    nc = build_nc(debug=False)
    res = run_bass_kernel_spmd(nc, in_maps, core_ids=list(range(8)))
    out = np.empty((B, L, D), np.float32)
    for c, r in enumerate(res.results):
        out[c // 4, (c % 4) * TPC:(c % 4 + 1) * TPC] = r["out"]
    return out
